# version 02
# baseline (speedup 1.0000x reference)
import math
import numpy as np
import ml_dtypes
import concourse.bass as bass
import concourse.mybir as mybir
from concourse.bass_utils import run_bass_kernel_spmd
from contextlib import ExitStack

F32 = mybir.dt.float32
BF16 = mybir.dt.bfloat16
U8 = mybir.dt.uint8
AF = mybir.ActivationFunctionType
ALU = mybir.AluOpType
AX = mybir.AxisListType
BF = ml_dtypes.bfloat16

H = 4096
S = 4096
R = 2048
NH, KVH, HD = 32, 8, 128
C = 512
RD = 512
I = 11008
NT = 32
IFT = 86
GUB = 43
DB = 16
STARTS_EVEN = [0, 12, 16, 28]
STARTS_ODD = [4, 8, 20, 24]
LIM = [8, 16, 24, 32]
M0 = [0, 8, 16, 24]
INVSQ = 1.0 / math.sqrt(HD)
LN_EPS = 1e-5
CW = 256

_DEBUG = []
_UNTIL = None


def _own_tiles(core):
    st = STARTS_EVEN if core % 2 == 0 else STARTS_ODD
    return [s + j for s in st for j in range(4)]


class Sem:
    def __init__(self, ctx, nc, name):
        self.h = ctx.enter_context(nc.semaphore(name))
        self.v = 0

    def inc(self, bi, n=1):
        bi.then_inc(self.h, n)
        self.v += n
        return bi


def build_program():
    nc = bass.Bass("TRN2", target_bir_lowering=False,
                   dynamic_dma_scratch_size=8192)

    def din(name, shape, dt):
        return nc.dram_tensor(name, list(shape), dt, kind="ExternalInput")

    xfT = din("xfT", (H, S), F32)
    xoT = din("xoT", (H, R), F32)
    wqP = din("wqP", (H, 4096), F32)
    woP = din("woP", (H, 4096), F32)
    wdkP = din("wdkP", (128, NT * C), F32)
    wkrP = din("wkrP", (128, NT * RD), F32)
    wupkP = din("wupkP", (128, 4 * C), F32)
    wupvP = din("wupvP", (128, 4 * 1024), F32)
    qguP = din("qguP", (GUB * 128, 16384), U8)
    sclguP = din("sclguP", (GUB, 4 * NT * 256), F32)
    qdP = din("qdP", (DB * 128, IFT * 256), U8)
    scldP = din("scldP", (DB, 2 * IFT * 256), F32)
    ckK = din("ckK", (128, S), BF16)
    skK = din("skK", (128, S), BF16)
    cqQ = din("cqQ", (128, R), BF16)
    sqQ = din("sqQ", (128, R), BF16)
    qposP = din("qposP", (1, R), F32)
    kcolP = din("kcolP", (128, NT), F32)

    outT = nc.dram_tensor("outT", [H, R], F32, kind="ExternalOutput")

    def scr(name, shape, dt):
        kind = "ExternalOutput" if name in _DEBUG else "Internal"
        return nc.dram_tensor(name, list(shape), dt, kind=kind)

    ckvS = scr("ckvS", (C, S), BF16)
    krS = scr("krS", (RD, S), BF16)
    ktS = scr("ktS", (KVH * 128, S), BF16)
    vS = scr("vS", (KVH * S, HD), BF16)
    qtS = scr("qtS", (4096, R), BF16)
    atS = scr("atS", (4096, R), BF16)
    stS = scr("stS", (H, R), BF16)
    hS = scr("hS", (I, R), BF16)
    wdS = scr("wdS", (NT * 128, I), BF16)
    xnS = scr("xnS", (H, R), BF16)
    xnbS = scr("xnbS", (H, S), BF16)
    statS = scr("statS", (16, 1024), F32)
    wgS = scr("wgS", (H, I), BF16)
    wuS = scr("wuS", (H, I), BF16)
    biasS = scr("biasS", (128, 128), F32)

    ctx = ExitStack()
    ld = Sem(ctx, nc, "ld")
    stA = Sem(ctx, nc, "stA")
    stB = Sem(ctx, nc, "stB")
    stC = Sem(ctx, nc, "stC")
    stD = Sem(ctx, nc, "stD")
    pe = Sem(ctx, nc, "pe")
    act = Sem(ctx, nc, "act")
    dve = Sem(ctx, nc, "dve")
    lsc = [Sem(ctx, nc, f"lsc{i}") for i in range(2)]
    lqg = [Sem(ctx, nc, f"lqg{i}") for i in range(2)]
    lw = [Sem(ctx, nc, f"lw{i}") for i in range(2)]
    stq = [Sem(ctx, nc, f"stq{i}") for i in range(2)]
    stw = [Sem(ctx, nc, f"stw{i}") for i in range(2)]
    stv = [Sem(ctx, nc, f"stv{i}") for i in range(2)]

    def _finish():
        with nc.Block("fin_dev") as blk:
            @blk.sync
            def _(sp):
                for s_ in [stA, stB, stC, stD, ld] + lsc + lqg + lw + stq + stw + stv:
                    sp.wait_ge(s_.h, s_.v)
        return nc

    def sb(name, shape, dt, stack):
        return stack.enter_context(nc.sbuf_tensor(name, list(shape), dt))

    def psf(name, shape, stack, dt=F32):
        return stack.enter_context(nc.psum_tensor(name, list(shape), dt))

    def load_bf16_via_f32(blk, dst, src_dram, stage, rows=slice(0, 128)):
        n = dst.shape[-1]
        half = stage.shape[-1]
        nchunks = (n + half - 1) // half
        for i in range(nchunks):
            c0, c1 = i * half, min((i + 1) * half, n)

            @blk.sync
            def _(sp, c0=c0, c1=c1):
                sp.wait_ge(dve.h, dve.v)
                ld.inc(sp.dma_start(stage[:, 0:c1 - c0],
                                    src_dram[rows, c0:c1]), 16)

            @blk.vector
            def _(v, c0=c0, c1=c1):
                v.wait_ge(ld.h, ld.v)
                dve.inc(v.tensor_copy(dst[:, c0:c1], stage[:, 0:c1 - c0]))

    g = ExitStack()
    ones_col = sb("ones_col", (128, 1), BF16, g)
    ones_row = sb("ones_row", (1, 128), BF16, g)
    ones_row_f = sb("ones_row_f", (1, 128), F32, g)
    ones128 = sb("ones128", (128, 128), BF16, g)
    norms = sb("norms", (1, 512), F32, g)
    bias_all = sb("bias_all", (128, 128), F32, g)
    cst = sb("cst", (1, 4), F32, g)
    cst128 = sb("cst128", (128, 1), F32, g)

    with nc.Block("init") as blk:
        @blk.vector
        def _(v):
            v.memset(ones_col[:, :], 1.0)
            v.memset(ones_row[:, :], 1.0)
            v.memset(ones_row_f[:, :], 1.0)
            v.memset(ones128[:, :], 1.0)
            v.memset(norms[:, :], 0.0)
            v.memset(cst[0:1, 0:1], LN_EPS)
            v.memset(cst128[:, :], LN_EPS)
            v.memset(cst[0:1, 1:2], math.log(1.02 * INVSQ))

    def emit_ln_chunk(blk, src_dram, c0, xf32, xch, sqt, m_b, i_b, ps_A, ps_B,
                      dst_fn):
        @blk.sync
        def _(sp):
            sp.wait_ge(dve.h, dve.v)
            for f in range(NT):
                ld.inc(sp.dma_start(xf32[:, f * CW:(f + 1) * CW],
                                    src_dram[f * 128:(f + 1) * 128, c0:c0 + CW]),
                       16)

        @blk.vector
        def _(v):
            v.wait_ge(ld.h, ld.v)
            for f in range(NT):
                bi = v.tensor_copy(xch[:, f * CW:(f + 1) * CW],
                                   xf32[:, f * CW:(f + 1) * CW])
            dve.inc(bi)

        cvt_dve = dve.v
        pe_b = pe.v
        act_b = act.v

        @blk.scalar
        def _(sc):
            sc.wait_ge(dve.h, cvt_dve)
            sc.wait_ge(pe.h, pe_b)
            for f in range(NT):
                act.inc(sc.activation(sqt[:, f * CW:(f + 1) * CW],
                                      xch[:, f * CW:(f + 1) * CW], AF.Square))

        @blk.tensor
        def _(t):
            t.wait_ge(dve.h, dve.v)
            for f in range(NT):
                t.matmul(ps_A[:, :], ones128[:, :],
                         xch[:, f * CW:(f + 1) * CW],
                         start=(f == 0), stop=(f == NT - 1))
            t.wait_ge(act.h, act_b + NT)
            for f in range(NT):
                pe.inc(t.matmul(ps_B[:, :], ones128[:, :],
                                sqt[:, f * CW:(f + 1) * CW],
                                start=(f == 0), stop=(f == NT - 1)))

        @blk.vector
        def _(v):
            v.wait_ge(pe.h, pe.v)
            dve.inc(v.tensor_scalar(m_b[:, :], ps_A[:, :], 1.0 / H, None,
                                    ALU.mult))
            dve.inc(v.tensor_scalar(i_b[:, :], ps_B[:, :], 1.0 / H, None,
                                    ALU.mult))
            v.wait_ge(dve.h, dve.v)
            dve.inc(v.scalar_tensor_tensor(sqt[:, 0:CW], m_b[:, :], -1.0,
                                           m_b[:, :], ALU.mult, ALU.mult))
            v.wait_ge(dve.h, dve.v)
            dve.inc(v.tensor_tensor(i_b[:, :], i_b[:, :], sqt[:, 0:CW],
                                    ALU.add))

        if "statS" in _DEBUG and src_dram.name == "xfT":
            @blk.sync
            def _(sp):
                sp.wait_ge(dve.h, dve.v)
                stA.inc(sp.dma_start(statS[c0 // CW:c0 // CW + 1, 0:256],
                                     m_b[0:1, :]), 16)
                stA.inc(sp.dma_start(statS[c0 // CW:c0 // CW + 1, 256:512],
                                     i_b[0:1, :]), 16)

        @blk.scalar
        def _(sc):
            sc.wait_ge(dve.h, dve.v)
            sc.wait_ge(stA.h, stA.v)
            act.inc(sc.activation(i_b[:, :], i_b[:, :], AF.Ln,
                                  bias=cst128[:, 0:1], scale=1.0))
            sc.wait_ge(act.h, act.v)
            act.inc(sc.activation(i_b[:, :], i_b[:, :], AF.Exp, scale=-0.5))

        if "statS" in _DEBUG and src_dram.name == "xfT":
            @blk.sync
            def _(sp):
                sp.wait_ge(act.h, act.v)
                stA.inc(sp.dma_start(statS[c0 // CW:c0 // CW + 1, 512:768],
                                     i_b[0:1, :]), 16)

            @blk.vector
            def _(v):
                v.wait_ge(stA.h, stA.v)
                dve.inc(v.tensor_copy(m_b[0:1, 0:1], m_b[0:1, 0:1]))

        @blk.vector
        def _(v):
            v.wait_ge(act.h, act.v)
            for f in range(NT):
                dst = dst_fn(f)
                dve.inc(v.tensor_tensor(dst, xch[:, f * CW:(f + 1) * CW],
                                        m_b[:, :], ALU.subtract))
                v.wait_ge(dve.h, dve.v)
                dve.inc(v.tensor_tensor(dst, dst, i_b[:, :], ALU.mult))

    p1 = ExitStack()
    wdk_sb = sb("wdk_sb", (128, NT * C), BF16, p1)
    wkr_sb = sb("wkr_sb", (128, NT * RD), BF16, p1)
    ck_sb = sb("ck_sb", (128, S), BF16, p1)
    sk_sb = sb("sk_sb", (128, S), BF16, p1)
    xf1 = sb("xf1", (128, NT * CW), F32, p1)
    xch1 = sb("xch1", (128, NT * CW), BF16, p1)
    xnch = sb("xnch", (128, NT * CW), BF16, p1)
    sqt1 = sb("sqt1", (128, NT * CW), BF16, p1)
    m_b1 = sb("m_b1", (128, CW), F32, p1)
    i_b1 = sb("i_b1", (128, CW), F32, p1)
    kvev = sb("kvev", (128, 4 * CW), BF16, p1)
    krev = sb("krev", (128, 4 * CW), BF16, p1)
    rp_t = sb("rp_t", (128, CW), BF16, p1)
    ps_A1 = psf("ps_A1", (128, CW), p1)
    ps_B1 = psf("ps_B1", (128, CW), p1)
    ps_kv = [psf(f"ps_kv{m}", (128, CW), p1) for m in range(4)]

    with nc.Block("p1_pre") as blk:
        @blk.sync
        def _(sp):
            ld.inc(sp.dma_start(ck_sb[:, :], ckK[:, :]), 16)
            ld.inc(sp.dma_start(sk_sb[:, :], skK[:, :]), 16)
        load_bf16_via_f32(blk, wdk_sb[:, :], wdkP, xf1)
        load_bf16_via_f32(blk, wkr_sb[:, :], wkrP, xf1)

    for ch in range(S // CW):
        c0 = ch * CW
        with nc.Block(f"p1_{ch}") as blk:
            emit_ln_chunk(blk, xfT, c0, xf1, xch1, sqt1, m_b1, i_b1,
                          ps_A1, ps_B1,
                          lambda f: xnch[:, f * CW:(f + 1) * CW])

        if "xnbS" in _DEBUG:
            with nc.Block(f"p1dbg_{ch}") as blk:
                @blk.sync
                def _(sp):
                    sp.wait_ge(dve.h, dve.v)
                    for f in range(NT):
                        stA.inc(sp.dma_start(
                            xnbS[f * 128:(f + 1) * 128, c0:c0 + CW],
                            xnch[:, f * CW:(f + 1) * CW]), 16)

        if _UNTIL == "p1ln":
            continue
        with nc.Block(f"p1kv_{ch}") as blk:
            @blk.tensor
            def _(t):
                t.wait_ge(dve.h, dve.v)
                for m in range(4):
                    for f in range(NT):
                        bi = t.matmul(
                            ps_kv[m][:, :],
                            wdk_sb[:, f * C + m * 128: f * C + (m + 1) * 128],
                            xnch[:, f * CW:(f + 1) * CW],
                            start=(f == 0), stop=(f == NT - 1))
                pe.inc(bi)

            @blk.vector
            def _(v):
                v.wait_ge(pe.h, pe.v)
                v.wait_ge(stA.h, stA.v)
                for m in range(4):
                    bi = v.tensor_copy(kvev[:, m * CW:(m + 1) * CW],
                                       ps_kv[m][:, :])
                dve.inc(bi)

            dve_kv = dve.v

            @blk.tensor
            def _(t):
                t.wait_ge(dve.h, dve_kv)
                for m in range(4):
                    for f in range(NT):
                        bi = t.matmul(
                            ps_kv[m][:, :],
                            wkr_sb[:, f * RD + m * 128: f * RD + (m + 1) * 128],
                            xnch[:, f * CW:(f + 1) * CW],
                            start=(f == 0), stop=(f == NT - 1))
                pe.inc(bi)

            @blk.sync
            def _(sp):
                sp.wait_ge(dve.h, dve_kv)
                for m in range(4):
                    stA.inc(sp.dma_start(
                        ckvS[m * 128:(m + 1) * 128, c0:c0 + CW],
                        kvev[:, m * CW:(m + 1) * CW]), 16)

            @blk.vector
            def _(v):
                v.wait_ge(pe.h, pe.v)
                v.wait_ge(stA.h, stA.v)
                for m in range(4):
                    for s32 in range(4):
                        q0 = s32 * 32
                        p0 = (s32 ^ 1) * 32
                        dst = krev[q0:q0 + 32, m * CW:(m + 1) * CW]
                        v.tensor_tensor(rp_t[q0:q0 + 32, :],
                                        ps_kv[m][p0:p0 + 32, :],
                                        sk_sb[q0:q0 + 32, c0:c0 + CW], ALU.mult)
                        dve.inc(v.tensor_tensor(dst, ps_kv[m][q0:q0 + 32, :],
                                                ck_sb[q0:q0 + 32, c0:c0 + CW],
                                                ALU.mult))
                        v.wait_ge(dve.h, dve.v)
                        dve.inc(v.tensor_tensor(dst, dst, rp_t[q0:q0 + 32, :],
                                                ALU.add))

            @blk.sync
            def _(sp):
                sp.wait_ge(dve.h, dve.v)
                for m in range(4):
                    stA.inc(sp.dma_start(
                        krS[m * 128:(m + 1) * 128, c0:c0 + CW],
                        krev[:, m * CW:(m + 1) * CW]), 16)

    p1.close()
    if _UNTIL in ("p1", "p1ln"):
        return _finish()

    p19 = ExitStack()
    ckv_sb = sb("ckv_sb", (128, 4 * S), BF16, p19)
    kr2_sb = sb("kr2_sb", (128, 4 * S), BF16, p19)
    wupk_sb = sb("wupk_sb", (128, 4 * C), BF16, p19)
    wupv_sb = sb("wupv_sb", (128, 4 * 1024), BF16, p19)
    kass = sb("kass", (128, S), BF16, p19)
    ksq = sb("ksq", (128, 2 * 512), BF16, p19)
    vev = sb("vev", (128, 2 * HD), BF16, p19)
    ps_kn = [psf(f"ps_kn{i}", (128, 512), p19) for i in range(2)]
    ps_v = [psf(f"ps_v{i}", (128, HD), p19) for i in range(2)]
    ps_nr = [psf(f"ps_nr{i}", (128, 512), p19) for i in range(2)]

    wst19 = sb("wst19", (128, 4096), F32, p19)
    with nc.Block("p19_pre") as blk:
        load_bf16_via_f32(blk, wupk_sb[:, :], wupkP, wst19)
        load_bf16_via_f32(blk, wupv_sb[:, :], wupvP, wst19)

        @blk.sync
        def _(sp):
            sp.wait_ge(stA.h, stA.v)
            for m in range(4):
                stA.inc(sp.dma_start(ckv_sb[:, m * S:(m + 1) * S],
                                     ckvS[m * 128:(m + 1) * 128, :]), 16)
                stA.inc(sp.dma_start(kr2_sb[:, m * S:(m + 1) * S],
                                     krS[m * 128:(m + 1) * 128, :]), 16)

    for h in range(KVH):
        with nc.Block(f"p19k_{h}") as blk:
            @blk.vector
            def _(v):
                v.wait_ge(stA.h, stA.v)
                v.wait_ge(ld.h, ld.v)
                b0 = 64 * (h % 2)
                hsl = slice((h // 2) * S, (h // 2 + 1) * S)
                v.tensor_copy(kass[0:32, :], kr2_sb[b0:b0 + 32, hsl])
                dve.inc(v.tensor_copy(kass[64:96, :],
                                      kr2_sb[b0 + 32:b0 + 64, hsl]))

            dve_cp = dve.v
            pe_b = pe.v

            @blk.tensor
            def _(t):
                t.wait_ge(stA.h, stA.v)
                t.wait_ge(ld.h, ld.v)
                mo = (h // 2) * 128 + (h % 2) * 64
                for nch in range(8):
                    if nch >= 2:
                        t.wait_ge(dve.h, dve_cp + (nch - 1))
                    for f in range(4):
                        bi = t.matmul(
                            ps_kn[nch % 2][0:64, :],
                            wupk_sb[:, f * C + mo: f * C + mo + 64],
                            ckv_sb[:, f * S + nch * 512: f * S + (nch + 1) * 512],
                            start=(f == 0), stop=(f == 3))
                        pe.inc(bi) if f == 3 else None

            @blk.vector
            def _(v):
                for nch in range(8):
                    v.wait_ge(pe.h, pe_b + nch + 1)
                    sl = slice(nch * 512, (nch + 1) * 512)
                    v.tensor_copy(kass[32:64, sl], ps_kn[nch % 2][0:32, :])
                    dve.inc(v.tensor_copy(kass[96:128, sl],
                                          ps_kn[nch % 2][32:64, :]))

            @blk.sync
            def _(sp):
                sp.wait_ge(dve.h, dve.v)
                stA.inc(sp.dma_start(ktS[h * 128:(h + 1) * 128, :], kass[:, :]),
                        16)

            act_b = act.v
            pe_b2 = pe.v
            dve_b2 = dve.v

            @blk.scalar
            def _(sc):
                sc.wait_ge(dve.h, dve_b2)
                for j in range(8):
                    if j >= 2:
                        sc.wait_ge(pe.h, pe_b2 + j - 1)
                    act.inc(sc.activation(ksq[:, (j % 2) * 512:(j % 2 + 1) * 512],
                                          kass[:, j * 512:(j + 1) * 512],
                                          AF.Square))

            @blk.tensor
            def _(t):
                for j in range(8):
                    t.wait_ge(act.h, act_b + j + 1)
                    if j >= 2:
                        t.wait_ge(dve.h, dve_b2 + j - 1)
                    pe.inc(t.matmul(ps_nr[j % 2][:, :], ones128[:, :],
                                    ksq[:, (j % 2) * 512:(j % 2 + 1) * 512],
                                    start=True, stop=True))

            @blk.vector
            def _(v):
                for j in range(8):
                    v.wait_ge(pe.h, pe_b2 + j + 1)
                    dve.inc(v.tensor_reduce(norms[0:1, 200 + j:201 + j],
                                            ps_nr[j % 2][0:1, :], AX.X, ALU.max))
                v.wait_ge(dve.h, dve.v)
                v.tensor_reduce(norms[0:1, h:h + 1],
                                norms[0:1, 200:208], AX.X, ALU.max)
                dve.inc(v.tensor_copy(norms[0:1, 220:221], norms[0:1, h:h + 1]))

        with nc.Block(f"p19v_{h}") as blk:
            stA_b = stA.v
            pe_b3 = pe.v
            dve_b3 = dve.v

            @blk.tensor
            def _(t):
                for kt in range(NT):
                    if kt >= 2:
                        t.wait_ge(dve.h, dve_b3 + (kt - 1))
                    for f in range(4):
                        bi = t.matmul(
                            ps_v[kt % 2][:, :],
                            ckv_sb[:, f * S + kt * 128: f * S + (kt + 1) * 128],
                            wupv_sb[:, f * 1024 + h * 128: f * 1024 + (h + 1) * 128],
                            start=(f == 0), stop=(f == 3))
                    pe.inc(bi)

            stv_b = [stv[0].v, stv[1].v]

            @blk.vector
            def _(v):
                for kt in range(NT):
                    v.wait_ge(pe.h, pe_b3 + kt + 1)
                    if kt >= 2:
                        v.wait_ge(stv[kt % 2].h, stv_b[kt % 2] + 16 * (kt // 2))
                    dve.inc(v.tensor_copy(
                        vev[:, (kt % 2) * HD:(kt % 2 + 1) * HD],
                        ps_v[kt % 2][:, :]))

            @blk.sync
            def _(sp):
                dve_b4 = dve.v - NT
                for kt in range(NT):
                    sp.wait_ge(dve.h, dve_b4 + kt + 1)
                    stv[kt % 2].inc(sp.dma_start(
                        vS[h * S + kt * 128: h * S + (kt + 1) * 128, :],
                        vev[:, (kt % 2) * HD:(kt % 2 + 1) * HD]), 16)

    p19.close()
    if _UNTIL == "p19":
        return _finish()

    gx = ExitStack()
    xn_own = sb("xn_own", (128, NT * R), BF16, gx)
    p15 = ExitStack()
    p15a = ExitStack()
    xf1b = sb("xf1b", (128, NT * CW), F32, p15a)
    xch2 = sb("xch2", (128, NT * CW), BF16, p15a)
    sqt2 = sb("sqt2", (128, NT * CW), BF16, p15a)
    m_b2 = sb("m_b2", (128, CW), F32, p15a)
    i_b2 = sb("i_b2", (128, CW), F32, p15a)
    ps_A2 = psf("ps_A2", (128, CW), p15a)
    ps_B2 = psf("ps_B2", (128, CW), p15a)

    for ch in range(R // CW):
        c0 = ch * CW
        with nc.Block(f"p15_{ch}") as blk:
            emit_ln_chunk(blk, xoT, c0, xf1b, xch2, sqt2, m_b2, i_b2,
                          ps_A2, ps_B2,
                          lambda f: xn_own[:, f * R + c0: f * R + c0 + CW])

    if "xnS" in _DEBUG:
        with nc.Block("p15dbg") as blk:
            @blk.sync
            def _(sp):
                sp.wait_ge(dve.h, dve.v)
                for f in range(NT):
                    stA.inc(sp.dma_start(xnS[f * 128:(f + 1) * 128, :],
                                         xn_own[:, f * R:(f + 1) * R]), 16)

    p15a.close()
    wq_sb = [sb(f"wq_sb{i}", (128, H), BF16, p15) for i in range(2)]
    q_ev = [sb(f"q_ev{i}", (128, 512), BF16, p15) for i in range(2)]
    q_t1 = sb("q_t1", (128, 512), BF16, p15)
    qsq2 = sb("qsq2", (128, 512), BF16, p15)
    cq_sb = sb("cq_sb", (128, R), BF16, p15)
    sq_sb = sb("sq_sb", (128, R), BF16, p15)
    ps_q = [psf(f"ps_q{i}", (128, 512), p15) for i in range(2)]
    ps_qn = [psf(f"ps_qn{i}", (128, 512), p15) for i in range(2)]

    wstq = [sb(f"wstq{i}", (128, H), F32, p15) for i in range(2)]
    with nc.Block("p15q_pre") as blk:
        @blk.sync
        def _(sp):
            ld.inc(sp.dma_start(cq_sb[:, :], cqQ[:, :]), 16)
            ld.inc(sp.dma_start(sq_sb[:, :], sqQ[:, :]), 16)
            ld.inc(sp.dma_start(wstq[0][:, :], wqP[0:128, :]), 16)

        @blk.vector
        def _(v):
            v.wait_ge(ld.h, ld.v)
            dve.inc(v.tensor_copy(wq_sb[0][:, :], wstq[0][:, :]))

    wq_cvt = {0: dve.v, 1: 0}

    for m in range(NT):
        with nc.Block(f"p15q_{m}") as blk:
            @blk.sync
            def _(sp):
                if m + 1 < NT:
                    sp.wait_ge(dve.h, dve.v)
                    ld.inc(sp.dma_start(wstq[(m + 1) % 2][:, :],
                                        wqP[(m + 1) * 128:(m + 2) * 128, :]), 16)

            if m + 1 < NT:
                @blk.vector
                def _(v):
                    v.wait_ge(ld.h, ld.v)
                    v.wait_ge(pe.h, pe.v)
                    dve.inc(v.tensor_copy(wq_sb[(m + 1) % 2][:, :],
                                          wstq[(m + 1) % 2][:, :]))
                wq_cvt[(m + 1) % 2] = dve.v
            pe_b = pe.v
            dve_b = dve.v
            act_b = act.v
            stq_b = [stq[0].v, stq[1].v]

            @blk.tensor
            def _(t):
                t.wait_ge(dve.h, wq_cvt[m % 2])
                for cch in range(4):
                    if cch >= 2:
                        t.wait_ge(dve.h, dve_b + 3 * (cch - 1))
                    for f in range(NT):
                        bi = t.matmul(
                            ps_q[cch % 2][:, :],
                            wq_sb[m % 2][:, f * 128:(f + 1) * 128],
                            xn_own[:, f * R + cch * 512: f * R + cch * 512 + 512],
                            start=(f == 0), stop=(f == NT - 1))
                    pe.inc(bi)

            @blk.vector
            def _(v):
                for cch in range(4):
                    v.wait_ge(pe.h, pe_b + cch + 1)
                    if cch >= 2:
                        v.wait_ge(stq[cch % 2].h, stq_b[cch % 2] + 16)
                    pq = ps_q[cch % 2]
                    qe = q_ev[cch % 2]
                    rsl = slice(cch * 512, cch * 512 + 512)
                    for s64 in range(2):
                        a0, b0 = s64 * 64, (1 - s64) * 64
                        v.tensor_tensor(q_t1[a0:a0 + 64, :], pq[b0:b0 + 64, :],
                                        sq_sb[a0:a0 + 64, rsl], ALU.mult)
                        dve.inc(v.tensor_tensor(qe[a0:a0 + 64, :],
                                                pq[a0:a0 + 64, :],
                                                cq_sb[a0:a0 + 64, rsl],
                                                ALU.mult))
                        v.wait_ge(dve.h, dve.v)
                        if s64 == 0:
                            v.tensor_tensor(qe[a0:a0 + 64, :], qe[a0:a0 + 64, :],
                                            q_t1[a0:a0 + 64, :], ALU.add)
                        else:
                            dve.inc(v.tensor_tensor(qe[a0:a0 + 64, :],
                                                    qe[a0:a0 + 64, :],
                                                    q_t1[a0:a0 + 64, :],
                                                    ALU.add))

            @blk.sync
            def _(sp):
                for cch in range(4):
                    sp.wait_ge(dve.h, dve_b + 3 * (cch + 1))
                    stq[cch % 2].inc(sp.dma_start(
                        qtS[m * 128:(m + 1) * 128, cch * 512:cch * 512 + 512],
                        q_ev[cch % 2][:, :]), 16)

            @blk.scalar
            def _(sc):
                for cch in range(4):
                    sc.wait_ge(dve.h, dve_b + 3 * (cch + 1))
                    if cch >= 1:
                        sc.wait_ge(pe.h, pe_b + 4 + cch)
                    act.inc(sc.activation(qsq2[:, :], q_ev[cch % 2][:, :],
                                          AF.Square))

            @blk.tensor
            def _(t):
                qn_dve_b = dve_b + 12
                for cch in range(4):
                    t.wait_ge(act.h, act_b + cch + 1)
                    if cch >= 2:
                        t.wait_ge(dve.h, qn_dve_b + cch - 1)
                    pe.inc(t.matmul(ps_qn[cch % 2][:, :], ones128[:, :],
                                    qsq2[:, :], start=True, stop=True))

            @blk.vector
            def _(v):
                pe_c = pe_b + 4
                for cch in range(4):
                    v.wait_ge(pe.h, pe_c + cch + 1)
                    dve.inc(v.tensor_reduce(
                        norms[0:1, 8 + m * 4 + cch: 9 + m * 4 + cch],
                        ps_qn[cch % 2][0:1, :], AX.X, ALU.max))


    ps_bb = psf("ps_bb", (128, 128), p15)
    with nc.Block("pbias") as blk:
        @blk.scalar
        def _(sc):
            sc.wait_ge(dve.h, dve.v)
            sc.activation(norms[0:1, 136:264], norms[0:1, 8:136], AF.Ln)
            act.inc(sc.activation(norms[0:1, 300:308], norms[0:1, 0:8], AF.Ln))

        @blk.vector
        def _(v):
            v.wait_ge(act.h, act.v)
            for h in range(8):
                bi = v.tensor_scalar(norms[0:1, 136 + 16 * h:136 + 16 * (h + 1)],
                                     norms[0:1, 136 + 16 * h:136 + 16 * (h + 1)],
                                     norms[0:1, 300 + h:301 + h], None, ALU.add)
            dve.inc(bi)

        @blk.scalar
        def _(sc):
            sc.wait_ge(dve.h, dve.v)
            act.inc(sc.activation(norms[0:1, 136:264], norms[0:1, 136:264],
                                  AF.Exp, scale=0.5, bias=cst[0:1, 1:2]))

        @blk.vector
        def _(v):
            v.wait_ge(act.h, act.v)
            dve.inc(v.tensor_scalar(norms[0:1, 136:264], norms[0:1, 136:264],
                                    -1.0, None, ALU.mult))

        @blk.tensor
        def _(t):
            t.wait_ge(dve.h, dve.v)
            pe.inc(t.matmul(ps_bb[:, :], ones_row_f[:, :], norms[0:1, 136:264],
                            start=True, stop=True))

        @blk.vector
        def _(v):
            v.wait_ge(pe.h, pe.v)
            dve.inc(v.tensor_copy(bias_all[:, :], ps_bb[:, :]))

        if "biasS" in _DEBUG:
            @blk.sync
            def _(sp):
                sp.wait_ge(dve.h, dve.v)
                stA.inc(sp.dma_start(biasS[:, :], bias_all[:, :]), 16)

    p15.close()
    if _UNTIL == "p15":
        return _finish()

    p3a = ExitStack()
    qgu_sb = [sb(f"qgu_sb{i}", (128, 16384), U8, p3a) for i in range(2)]
    scl_sb = [sb(f"scl_sb{i}", (1, 1024), F32, p3a) for i in range(2)]
    wg_sb = sb("wg_sb", (128, NT * 256), BF16, p3a)
    wu_sb = sb("wu_sb", (128, NT * 256), BF16, p3a)
    dq_t = sb("dq_t", (128, 256), BF16, p3a)
    sg_tmp = sb("sg_tmp", (128, R), BF16, p3a)
    h_ev = sb("h_ev", (128, R), BF16, p3a)
    ps_gu = psf("ps_gu", (128, R), p3a)
    ps_z = psf("ps_z", (128, 256), p3a)
    ps_s = psf("ps_s", (128, 256), p3a)
    ps_z2 = psf("ps_z2a", (128, 256), p3a)
    ps_s2 = psf("ps_s2a", (128, 256), p3a)

    dq_done = [0, 0]

    with nc.Block("p3a_pre") as blk:
        @blk.sync
        def _(sp):
            lqg[0].inc(sp.dma_start(qgu_sb[0][:, :], qguP[0:128, :]), 16)

    for b in range(GUB):
        bi_ = b % 2
        with nc.Block(f"p3a_{b}") as blk:
            dve_b = dve.v
            scl_at = []

            @blk.sync
            def _(sp):
                for gi in range(NT):
                    if gi >= 2:
                        sp.wait_ge(dve.h, dve_b + 4 * (gi - 1))
                    lsc[gi % 2].inc(sp.dma_start(
                        scl_sb[gi % 2][0:1, :],
                        sclguP[b:b + 1, gi * 1024:(gi + 1) * 1024]), 16)
                    scl_at.append(lsc[gi % 2].v)
                if b + 1 < GUB:
                    sp.wait_ge(dve.h, dq_done[(b + 1) % 2])
                    lqg[(b + 1) % 2].inc(sp.dma_start(
                        qgu_sb[(b + 1) % 2][:, :],
                        qguP[(b + 1) * 128:(b + 2) * 128, :]), 16)

            pe_prev = pe.v
            pe_b = pe.v

            @blk.tensor
            def _(t):
                t.wait_ge(pe.h, pe_prev)
                for f in range(NT):
                    t.wait_ge(lsc[f % 2].h, scl_at[f])
                    if f >= 1:
                        t.wait_ge(dve.h, dve_b + 4 * f)
                    sc_ = scl_sb[f % 2]
                    pe.inc(t.matmul(ps_z[:, :], ones_row_f[:, :],
                                    sc_[0:1, 0:256],
                                    start=True, stop=True))
                    pe.inc(t.matmul(ps_s[:, :], ones_row_f[:, :],
                                    sc_[0:1, 256:512],
                                    start=True, stop=True))
                    pe.inc(t.matmul(ps_z2[:, :], ones_row_f[:, :],
                                    sc_[0:1, 512:768],
                                    start=True, stop=True))
                    pe.inc(t.matmul(ps_s2[:, :], ones_row_f[:, :],
                                    sc_[0:1, 768:1024],
                                    start=True, stop=True))

            @blk.vector
            def _(v):
                v.wait_ge(pe.h, pe_prev)
                v.wait_ge(lqg[bi_].h, lqg[bi_].v)
                for f in range(NT):
                    v.wait_ge(pe.h, pe_b + 4 * f + 2)
                    qg_ = qgu_sb[bi_][:, f * 512:f * 512 + 256]
                    dve.inc(v.tensor_tensor(dq_t[:, :], qg_, ps_z[:, :],
                                            ALU.subtract))
                    v.wait_ge(dve.h, dve.v)
                    dve.inc(v.tensor_tensor(wg_sb[:, f * 256:(f + 1) * 256],
                                            dq_t[:, :], ps_s[:, :], ALU.mult))
                    v.wait_ge(pe.h, pe_b + 4 * f + 4)
                    qu_ = qgu_sb[bi_][:, f * 512 + 256:f * 512 + 512]
                    dve.inc(v.tensor_tensor(dq_t[:, :], qu_, ps_z2[:, :],
                                            ALU.subtract))
                    v.wait_ge(dve.h, dve.v)
                    dve.inc(v.tensor_tensor(wu_sb[:, f * 256:(f + 1) * 256],
                                            dq_t[:, :], ps_s2[:, :], ALU.mult))

            dq_done[bi_] = dve.v
            dq_v = dve.v

            if "wgS" in _DEBUG:
                @blk.sync
                def _(sp, b=b):
                    sp.wait_ge(dve.h, dq_v)
                    for f in range(NT):
                        stB.inc(sp.dma_start(
                            wgS[f * 128:(f + 1) * 128, b * 256:(b + 1) * 256],
                            wg_sb[:, f * 256:(f + 1) * 256]), 16)
                        stB.inc(sp.dma_start(
                            wuS[f * 128:(f + 1) * 128, b * 256:(b + 1) * 256],
                            wu_sb[:, f * 256:(f + 1) * 256]), 16)

            for msub in range(2):
                irow = b * 256 + msub * 128
                for mat, wsrc in ((0, wg_sb), (1, wu_sb)):
                    @blk.tensor
                    def _(t, mat=mat, wsrc=wsrc, msub=msub):
                        t.wait_ge(dve.h, dq_v)
                        if mat == 0:
                            t.wait_ge(dve.h, dve.v)
                        else:
                            t.wait_ge(act.h, act.v)
                        for f in range(NT):
                            for nch in range(4):
                                bi2 = t.matmul(
                                    ps_gu[:, nch * 512:(nch + 1) * 512],
                                    wsrc[:, f * 256 + msub * 128:
                                         f * 256 + msub * 128 + 128],
                                    xn_own[:, f * R + nch * 512:
                                           f * R + (nch + 1) * 512],
                                    start=(f == 0), stop=(f == NT - 1))
                        pe.inc(bi2)

                    if mat == 0:
                        @blk.scalar
                        def _(sc):
                            sc.wait_ge(pe.h, pe.v)
                            act.inc(sc.activation(sg_tmp[:, :], ps_gu[:, :],
                                                  AF.Silu))
                    else:
                        @blk.vector
                        def _(v):
                            v.wait_ge(pe.h, pe.v)
                            v.wait_ge(stB.h, stB.v)
                            dve.inc(v.tensor_tensor(h_ev[:, :], sg_tmp[:, :],
                                                    ps_gu[:, :], ALU.mult))

                @blk.sync
                def _(sp, irow=irow):
                    sp.wait_ge(dve.h, dve.v)
                    stB.inc(sp.dma_start(hS[irow:irow + 128, :], h_ev[:, :]), 16)

    p3a.close()
    gx.close()
    if _UNTIL == "p3a":
        return _finish()

    p2 = ExitStack()
    kt_sb = [sb(f"kt_sb{i}", (128, S), BF16, p2) for i in range(2)]
    vh_sb = [sb(f"vh_sb{i}", (128, S), BF16, p2) for i in range(2)]
    qt_sb = [sb(f"qt_sb{i}", (128, 512), BF16, p2) for i in range(2)]
    pt_sb = [sb(f"pt_sb{i}", (128, 512), BF16, p2) for i in range(2)]
    mask_sb = sb("mask_sb", (128, 8 * 512), BF16, p2)
    at_sb = sb("at_sb", (128, NH * 512), BF16, p2)
    il_sb = sb("il_sb", (128, 512), F32, p2)
    qpos_sb = sb("qpos_sb", (1, R), F32, p2)
    kcol_sb = sb("kcol_sb", (128, NT), F32, p2)
    ps_st = [psf(f"ps_st{i}", (128, 512), p2) for i in range(3)]
    ps_o = psf("ps_o", (128, 512), p2)
    ps_l = psf("ps_l", (128, 512), p2)
    ps_x = psf("ps_x", (128, 512), p2)

    with nc.Block("p2_pre") as blk:
        @blk.sync
        def _(sp):
            ld.inc(sp.dma_start(qpos_sb[:, :], qposP[:, :]), 16)
            ld.inc(sp.dma_start(kcol_sb[:, :], kcolP[:, :]), 16)

    for cch in range(4):
        L = LIM[cch]
        m0 = M0[cch]
        with nc.Block(f"p2m_{cch}") as blk:
            @blk.tensor
            def _(t):
                t.wait_ge(ld.h, ld.v)
                t.wait_ge(dve.h, dve.v)
                pe.inc(t.matmul(ps_x[:, :], ones_row_f[:, :],
                                qpos_sb[0:1, cch * 512:(cch + 1) * 512],
                                start=True, stop=True))

            @blk.vector
            def _(v):
                v.wait_ge(pe.h, pe.v)
                v.wait_ge(stB.h, stB.v)
                for kt in range(m0, L):
                    bi = v.tensor_scalar(
                        mask_sb[:, (kt - m0) * 512:(kt - m0 + 1) * 512],
                        ps_x[:, :], kcol_sb[:, kt:kt + 1], None, ALU.is_ge)
                dve.inc(bi)

        for kvh in range(KVH):
            kbi = kvh % 2
            with nc.Block(f"p2kv_{cch}_{kvh}") as blk:
                @blk.sync
                def _(sp):
                    sp.wait_ge(pe.h, pe.v)
                    stB.inc(sp.dma_start(
                        kt_sb[kbi][:, 0:L * 128],
                        ktS[kvh * 128:(kvh + 1) * 128, 0:L * 128]), 16)
                    for kt in range(L):
                        stB.inc(sp.dma_start(
                            vh_sb[kbi][:, kt * 128:(kt + 1) * 128],
                            vS[kvh * S + kt * 128: kvh * S + (kt + 1) * 128, :]),
                            16)

            for qh in range(4):
                head = kvh * 4 + qh
                qbi = head % 2
                slot = head * 4 + cch
                with nc.Block(f"p2h_{cch}_{head}") as blk:
                    @blk.sync
                    def _(sp):
                        sp.wait_ge(pe.h, pe.v)
                        stB.inc(sp.dma_start(
                            qt_sb[qbi][:, :],
                            qtS[head * 128:(head + 1) * 128,
                                cch * 512:(cch + 1) * 512]), 16)

                    pe_b = pe.v
                    act_b = act.v
                    dve_b = dve.v

                    def st_done(kt):
                        return pe_b + (1 if kt == 0 else 3 * kt - 1)

                    def pv_done(kt):
                        return pe_b + 3 * (kt + 1)

                    @blk.tensor
                    def _(t):
                        t.wait_ge(stB.h, stB.v)
                        t.wait_ge(dve.h, dve_b)
                        for it in range(L + 1):
                            if it < L:
                                kt = it
                                if kt >= 3:
                                    t.wait_ge(act.h, act_b + kt - 2)
                                pe.inc(t.matmul(
                                    ps_st[kt % 3][:, :],
                                    kt_sb[kbi][:, kt * 128:(kt + 1) * 128],
                                    qt_sb[qbi][:, :], start=True, stop=True))
                            if it >= 1:
                                kt = it - 1
                                if kt < m0:
                                    t.wait_ge(act.h, act_b + kt + 1)
                                else:
                                    t.wait_ge(dve.h, dve_b + (kt - m0) + 1)
                                pe.inc(t.matmul(
                                    ps_o[:, :],
                                    vh_sb[kbi][:, kt * 128:(kt + 1) * 128],
                                    pt_sb[kt % 2][:, :],
                                    start=(kt == 0), stop=(kt == L - 1)))
                                pe.inc(t.matmul(
                                    ps_l[:, :], ones128[:, :],
                                    pt_sb[kt % 2][:, :],
                                    start=(kt == 0), stop=(kt == L - 1)))

                    @blk.scalar
                    def _(sc):
                        for kt in range(L):
                            sc.wait_ge(pe.h, st_done(kt))
                            if kt >= 2:
                                sc.wait_ge(pe.h, pv_done(kt - 2))
                            act.inc(sc.activation(
                                pt_sb[kt % 2][:, :], ps_st[kt % 3][:, :],
                                AF.Exp, bias=bias_all[:, slot:slot + 1],
                                scale=INVSQ))

                    @blk.vector
                    def _(v):
                        for kt in range(m0, L):
                            v.wait_ge(act.h, act_b + kt + 1)
                            dve.inc(v.tensor_tensor(
                                pt_sb[kt % 2][:, :], pt_sb[kt % 2][:, :],
                                mask_sb[:, (kt - m0) * 512:(kt - m0 + 1) * 512],
                                ALU.mult))

                    pe_tail = pe_b + 3 * L

                    @blk.vector
                    def _(v):
                        v.wait_ge(pe.h, pe_tail)
                        dve.inc(v.reciprocal(il_sb[:, :], ps_l[:, :]))
                        v.wait_ge(dve.h, dve.v)
                        dve.inc(v.tensor_tensor(
                            at_sb[:, head * 512:(head + 1) * 512], ps_o[:, :],
                            il_sb[:, :], ALU.mult))

        with nc.Block(f"p2st_{cch}") as blk:
            @blk.sync
            def _(sp):
                sp.wait_ge(dve.h, dve.v)
                for hh in range(NH):
                    stB.inc(sp.dma_start(
                        atS[hh * 128:(hh + 1) * 128, cch * 512:(cch + 1) * 512],
                        at_sb[:, hh * 512:(hh + 1) * 512]), 16)

    p2.close()
    if _UNTIL == "p2":
        return _finish()

    p25 = ExitStack()
    at2_sb = sb("at2_sb", (128, NH * R), BF16, p25)
    wo_sb = [sb(f"wo_sb{i}", (128, H), BF16, p25) for i in range(2)]
    wsto = [sb(f"wsto{i}", (128, H), F32, p25) for i in range(2)]
    st_ev = [sb(f"st_ev{i}", (128, 512), BF16, p25) for i in range(2)]
    ps_w = [psf(f"ps_w{i}", (128, 512), p25) for i in range(2)]

    with nc.Block("p25_pre") as blk:
        @blk.sync
        def _(sp):
            sp.wait_ge(stB.h, stB.v)
            for hh in range(NH):
                stC.inc(sp.dma_start(at2_sb[:, hh * R:(hh + 1) * R],
                                     atS[hh * 128:(hh + 1) * 128, :]), 16)

        @blk.scalar
        def _(sc):
            ld.inc(sc.dma_start(wsto[0][:, :], woP[0:128, :]), 16)

        @blk.vector
        def _(v):
            v.wait_ge(ld.h, ld.v)
            dve.inc(v.tensor_copy(wo_sb[0][:, :], wsto[0][:, :]))

    wo_cvt = {0: dve.v, 1: 0}

    for m in range(NT):
        with nc.Block(f"p25_{m}") as blk:
            @blk.scalar
            def _(sc):
                if m + 1 < NT:
                    sc.wait_ge(dve.h, dve.v)
                    ld.inc(sc.dma_start(wsto[(m + 1) % 2][:, :],
                                        woP[(m + 1) * 128:(m + 2) * 128, :]), 16)

            if m + 1 < NT:
                @blk.vector
                def _(v):
                    v.wait_ge(ld.h, ld.v)
                    v.wait_ge(pe.h, pe.v)
                    dve.inc(v.tensor_copy(wo_sb[(m + 1) % 2][:, :],
                                          wsto[(m + 1) % 2][:, :]))
                wo_cvt[(m + 1) % 2] = dve.v

            pe_b = pe.v
            dve_b = dve.v
            stC_b = stC.v

            @blk.tensor
            def _(t):
                t.wait_ge(dve.h, wo_cvt[m % 2])
                t.wait_ge(stC.h, stC_b)
                for cch in range(4):
                    if cch >= 2:
                        t.wait_ge(dve.h, dve_b + cch - 1)
                    for f in range(NH):
                        bi = t.matmul(
                            ps_w[cch % 2][:, :],
                            wo_sb[m % 2][:, f * 128:(f + 1) * 128],
                            at2_sb[:, f * R + cch * 512: f * R + cch * 512 + 512],
                            start=(f == 0), stop=(f == NH - 1))
                    pe.inc(bi)

            stw_b = [stw[0].v, stw[1].v]

            @blk.vector
            def _(v):
                for cch in range(4):
                    v.wait_ge(pe.h, pe_b + cch + 1)
                    if cch >= 2:
                        v.wait_ge(stw[cch % 2].h, stw_b[cch % 2] + 16)
                    dve.inc(v.tensor_copy(st_ev[cch % 2][:, :],
                                          ps_w[cch % 2][:, :]))

            @blk.sync
            def _(sp):
                for cch in range(4):
                    sp.wait_ge(dve.h, dve_b + cch + 1)
                    stw[cch % 2].inc(sp.dma_start(
                        stS[m * 128:(m + 1) * 128, cch * 512:(cch + 1) * 512],
                        st_ev[cch % 2][:, :]), 16)

    p25.close()
    if _UNTIL == "p25":
        return _finish()

    p3d = ExitStack()
    GD = 4
    NGD = (IFT + GD - 1) // GD
    qd_sb = [sb(f"qd_sb{i}", (128, IFT * 256), U8, p3d) for i in range(2)]
    scd_sb = [sb(f"scd_sb{i}", (1, GD * 512), F32, p3d) for i in range(2)]
    wd_ev = sb("wd_ev", (128, IFT * 256), BF16, p3d)
    dq_t2 = sb("dq_t2", (128, 256), BF16, p3d)
    ps_sd = psf("ps_sd", (128, 256), p3d)
    ps_zd = psf("ps_zd", (128, 256), p3d)

    dqd_done = [0, 0]
    with nc.Block("p3d_pre") as blk:
        @blk.sync
        def _(sp):
            lqg[0].inc(sp.dma_start(qd_sb[0][:, :], qdP[0:128, :]), 16)

    for b in range(DB):
        bi_ = b % 2
        with nc.Block(f"p3d_{b}") as blk:
            dve_b = dve.v
            scl_at = []

            @blk.sync
            def _(sp):
                for gi in range(NGD):
                    nf = min(GD, IFT - gi * GD)
                    if gi >= 2:
                        sp.wait_ge(dve.h, dve_b + 2 * GD * (gi - 1))
                    lsc[gi % 2].inc(sp.dma_start(
                        scd_sb[gi % 2][0:1, 0:nf * 512],
                        scldP[b:b + 1, gi * GD * 512: gi * GD * 512 + nf * 512]),
                        16)
                    scl_at.append(lsc[gi % 2].v)
                if b + 1 < DB:
                    sp.wait_ge(dve.h, dqd_done[(b + 1) % 2])
                    lqg[(b + 1) % 2].inc(sp.dma_start(
                        qd_sb[(b + 1) % 2][:, :],
                        qdP[(b + 1) * 128:(b + 2) * 128, :]), 16)

            pe_b = pe.v
            stC_b = stC.v

            @blk.tensor
            def _(t):
                t.wait_ge(stC.h, stC_b)
                for f in range(IFT):
                    t.wait_ge(lsc[(f // GD) % 2].h, scl_at[f // GD])
                    if f >= 1:
                        t.wait_ge(dve.h, dve_b + 2 * f)
                    off = (f % GD) * 512
                    sc_ = scd_sb[(f // GD) % 2]
                    pe.inc(t.matmul(ps_zd[:, :], ones_row_f[:, :],
                                    sc_[0:1, off:off + 256],
                                    start=True, stop=True))
                    pe.inc(t.matmul(ps_sd[:, :], ones_row_f[:, :],
                                    sc_[0:1, off + 256:off + 512],
                                    start=True, stop=True))

            @blk.vector
            def _(v):
                v.wait_ge(stC.h, stC_b)
                v.wait_ge(lqg[bi_].h, lqg[bi_].v)
                for f in range(IFT):
                    v.wait_ge(pe.h, pe_b + 2 * f + 2)
                    qsl = qd_sb[bi_][:, f * 256:(f + 1) * 256]
                    dve.inc(v.tensor_tensor(dq_t2[:, :], qsl, ps_zd[:, :],
                                            ALU.subtract))
                    v.wait_ge(dve.h, dve.v)
                    dve.inc(v.tensor_tensor(wd_ev[:, f * 256:(f + 1) * 256],
                                            dq_t2[:, :], ps_sd[:, :], ALU.mult))

            dqd_done[bi_] = dve.v
            dve_b2 = dve.v

            @blk.sync
            def _(sp):
                sp.wait_ge(dve.h, dve_b2)
                for msub in range(2):
                    m = b * 2 + msub
                    for f in range(IFT):
                        stC.inc(sp.dma_start(
                            wdS[m * 128:(m + 1) * 128, f * 128:(f + 1) * 128],
                            wd_ev[:, f * 256 + msub * 128:
                                  f * 256 + msub * 128 + 128]), 16)

    p3d.close()
    if _UNTIL == "p3d":
        return _finish()

    p3b = ExitStack()
    h_sb = sb("h_sb", (128, IFT * 512), BF16, p3b)
    wd_sb = [sb(f"wd_sb{i}", (128, I), BF16, p3b) for i in range(2)]
    st_in = [sb(f"st_in{i}", (128, 512), BF16, p3b) for i in range(2)]
    x_in = [sb(f"x_in{i}", (128, 512), F32, p3b) for i in range(2)]
    gf_t = sb("gf_t", (128, 512), BF16, p3b)
    o_ev = [sb(f"o_ev{i}", (128, 512), F32, p3b) for i in range(2)]
    ps_d = [psf(f"ps_d{i}", (128, 512), p3b) for i in range(2)]

    for cch in range(4):
        with nc.Block(f"p3b_{cch}") as blk:
            @blk.sync
            def _(sp):
                sp.wait_ge(pe.h, pe.v)
                for f in range(IFT):
                    stD.inc(sp.dma_start(
                        h_sb[:, f * 512:(f + 1) * 512],
                        hS[f * 128:(f + 1) * 128, cch * 512:(cch + 1) * 512]),
                        16)

            @blk.scalar
            def _(sc):
                sc.wait_ge(pe.h, pe.v)
                lw[0].inc(sc.dma_start(wd_sb[0][:, :], wdS[0:128, :]), 16)
                lw[0].inc(sc.dma_start(st_in[0][:, :],
                                       stS[0:128, cch * 512:(cch + 1) * 512]), 16)
                lw[0].inc(sc.dma_start(x_in[0][:, :],
                                       xoT[0:128, cch * 512:(cch + 1) * 512]), 16)

        for m in range(NT):
            with nc.Block(f"p3b_{cch}_{m}") as blk:
                @blk.scalar
                def _(sc):
                    if m + 1 < NT:
                        sc.wait_ge(pe.h, pe.v)
                        par = (m + 1) % 2
                        lw[par].inc(sc.dma_start(
                            wd_sb[par][:, :],
                            wdS[(m + 1) * 128:(m + 2) * 128, :]), 16)
                        lw[par].inc(sc.dma_start(
                            st_in[par][:, :],
                            stS[(m + 1) * 128:(m + 2) * 128,
                                cch * 512:(cch + 1) * 512]), 16)
                        lw[par].inc(sc.dma_start(
                            x_in[par][:, :],
                            xoT[(m + 1) * 128:(m + 2) * 128,
                                cch * 512:(cch + 1) * 512]), 16)

                lw_need = lw[m % 2].v

                @blk.tensor
                def _(t):
                    t.wait_ge(lw[m % 2].h, lw_need)
                    t.wait_ge(stD.h, stD.v)
                    t.wait_ge(act.h, act.v)
                    for f in range(IFT):
                        bi = t.matmul(ps_d[m % 2][:, :],
                                      wd_sb[m % 2][:, f * 128:(f + 1) * 128],
                                      h_sb[:, f * 512:(f + 1) * 512],
                                      start=(f == 0), stop=(f == IFT - 1))
                    pe.inc(bi)

                @blk.scalar
                def _(sc):
                    sc.wait_ge(pe.h, pe.v)
                    sc.wait_ge(dve.h, dve.v)
                    act.inc(sc.activation(gf_t[:, :], ps_d[m % 2][:, :],
                                          AF.Silu))

                @blk.vector
                def _(v):
                    v.wait_ge(act.h, act.v)
                    v.wait_ge(stD.h, stD.v)
                    v.tensor_tensor(gf_t[:, :], st_in[m % 2][:, :], gf_t[:, :],
                                    ALU.mult)
                    dve.inc(v.tensor_tensor(o_ev[m % 2][:, :], x_in[m % 2][:, :],
                                            gf_t[:, :], ALU.add))

                @blk.sync
                def _(sp):
                    sp.wait_ge(dve.h, dve.v)
                    stD.inc(sp.dma_start(
                        outT[m * 128:(m + 1) * 128, cch * 512:(cch + 1) * 512],
                        o_ev[m % 2][:, :]), 16)

    with nc.Block("fin") as blk:
        @blk.sync
        def _(sp):
            sp.wait_ge(stD.h, stD.v)
            sp.wait_ge(ld.h, ld.v)
            for s_ in lsc + lqg + lw + stq + stw + stv:
                sp.wait_ge(s_.h, s_.v)

    p3b.close()
    g.close()
    ctx.close()
    return nc


def _prep_core_inputs(core, x, w):
    b = core // 2
    tiles = _own_tiles(core)
    own = np.concatenate([np.arange(t * 128, (t + 1) * 128) for t in tiles])
    xb = x[b]
    xfT = np.ascontiguousarray(xb.T)
    xoT = np.ascontiguousarray(xfT[:, own])

    inv_freq = 1.0 / 10000.0 ** (np.arange(0, HD, 2, dtype=np.float64) / HD)
    posk = np.arange(S, dtype=np.float64)
    pk = np.outer(inv_freq[np.arange(128) % 32], posk)
    ckK = np.cos(pk)
    skK = np.sin(pk) * np.where((np.arange(128) % 64) < 32, -1.0, 1.0)[:, None]
    qp = own.astype(np.float64)
    pq = np.outer(inv_freq[np.arange(128) % 64], qp)
    cqQ = np.cos(pq)
    sqQ = np.sin(pq) * np.where(np.arange(128) < 64, -1.0, 1.0)[:, None]

    d = dict(
        xfT=xfT, xoT=xoT,
        ckK=ckK.astype(BF), skK=skK.astype(BF),
        cqQ=cqQ.astype(BF), sqQ=sqQ.astype(BF),
        qposP=own.astype(np.float32)[None, :],
        kcolP=(np.arange(128)[:, None] + 128 * np.arange(NT)[None, :])
        .astype(np.float32),
    )
    d.update(w)
    return d, own


def _pm_tiles(WT, fm, mm):
    a = WT.reshape(fm, 128, mm, 128).transpose(1, 0, 2, 3)
    return np.ascontiguousarray(a.reshape(128, fm * mm * 128))


def _prep_weights(inp):
    Wq = np.asarray(inp["L0_attn_Wq"], np.float32)
    Wkr = np.asarray(inp["L0_attn_Wkr"], np.float32)
    Wdk = np.asarray(inp["L0_attn_Wdk"], np.float32)
    Wupk = np.asarray(inp["L0_attn_Wupk"], np.float32)
    Wupv = np.asarray(inp["L0_attn_Wupv"], np.float32)
    Wo = np.asarray(inp["L0_attn_Wo"], np.float32)

    wqP = np.ascontiguousarray(
        Wq.T.reshape(NT, 128, NT, 128).transpose(2, 1, 0, 3).reshape(4096, 4096))
    woP = np.ascontiguousarray(
        Wo.T.reshape(NT, 128, NT, 128).transpose(2, 1, 0, 3).reshape(4096, 4096))
    wdkP = _pm_tiles(np.ascontiguousarray(Wdk.T), NT, 4)
    wkrP = _pm_tiles(np.ascontiguousarray(Wkr.T), NT, 4)
    wupkP = _pm_tiles(np.ascontiguousarray(Wupk.T), 4, 4)
    wupvP = _pm_tiles(np.ascontiguousarray(Wupv.T), 4, 8)

    qg = np.asarray(inp["L0_ffn_qw_gate"]).astype(np.uint8).T
    qu = np.asarray(inp["L0_ffn_qw_up"]).astype(np.uint8).T
    qgP = qg.reshape(NT, 128, GUB, 256).transpose(2, 1, 0, 3)
    quP = qu.reshape(NT, 128, GUB, 256).transpose(2, 1, 0, 3)
    qguP = np.ascontiguousarray(
        np.stack([qgP, quP], axis=3).reshape(GUB * 128, 16384))

    def rows_T(a, blocks):
        return np.asarray(a, np.float32).T.reshape(NT, blocks, 256) \
            .transpose(1, 0, 2)

    zgT = rows_T(inp["L0_ffn_z_gate"], GUB)
    sgT = rows_T(inp["L0_ffn_s_gate"], GUB)
    zuT = rows_T(inp["L0_ffn_z_up"], GUB)
    suT = rows_T(inp["L0_ffn_s_up"], GUB)
    sclguP = np.ascontiguousarray(
        np.stack([zgT, sgT, zuT, suT], axis=2).reshape(GUB, 4 * NT * 256))

    qd = np.asarray(inp["L0_ffn_qw_down"]).astype(np.uint8).T
    qdP = np.ascontiguousarray(
        qd.reshape(IFT, 128, DB, 256).transpose(2, 1, 0, 3)
        .reshape(DB * 128, IFT * 256))
    zdT = np.asarray(inp["L0_ffn_z_down"], np.float32).T \
        .reshape(IFT, DB, 256).transpose(1, 0, 2)
    sdT = np.asarray(inp["L0_ffn_s_down"], np.float32).T \
        .reshape(IFT, DB, 256).transpose(1, 0, 2)
    scldP = np.ascontiguousarray(
        np.stack([zdT, sdT], axis=2).reshape(DB, 2 * IFT * 256))

    return dict(wqP=wqP, woP=woP, wdkP=wdkP, wkrP=wkrP, wupkP=wupkP,
                wupvP=wupvP, qguP=qguP, sclguP=sclguP, qdP=qdP, scldP=scldP)


def kernel(**inputs):
    x = np.asarray(inputs["L0_attn_hidden_states"], dtype=np.float32)
    w = _prep_weights(inputs)

    in_maps = []
    owns = []
    for core in range(8):
        d, own = _prep_core_inputs(core, x, w)
        in_maps.append(d)
        owns.append(own)

    nc = build_program()
    res = run_bass_kernel_spmd(nc, in_maps, core_ids=list(range(8)))

    out = np.empty((4, S, H), dtype=np.float32)
    for core in range(8):
        out[core // 2, owns[core], :] = res.results[core]["outT"].T
    return out


# version 04
# speedup vs baseline: 1.1957x; 1.1957x over previous
import math
import numpy as np
import ml_dtypes
import concourse.bass as bass
import concourse.mybir as mybir
from concourse.bass_utils import run_bass_kernel_spmd
from contextlib import ExitStack

F32 = mybir.dt.float32
BF16 = mybir.dt.bfloat16
U8 = mybir.dt.uint8
AF = mybir.ActivationFunctionType
ALU = mybir.AluOpType
AX = mybir.AxisListType
BF = ml_dtypes.bfloat16

H = 4096
S = 4096
R = 2048
NH, KVH, HD = 32, 8, 128
C = 512
RD = 512
I = 11008
NT = 32
IFT = 86
GUB = 43
DB = 16
STARTS_EVEN = [0, 12, 16, 28]
STARTS_ODD = [4, 8, 20, 24]
LIM = [8, 16, 24, 32]
M0 = [0, 8, 16, 24]
INVSQ = 1.0 / math.sqrt(HD)
LN_EPS = 1e-5
CW = 256

_DEBUG = []
_UNTIL = None


def _own_tiles(core):
    st = STARTS_EVEN if core % 2 == 0 else STARTS_ODD
    return [s + j for s in st for j in range(4)]


class Sem:
    def __init__(self, ctx, nc, name):
        self.h = ctx.enter_context(nc.semaphore(name))
        self.v = 0

    def inc(self, bi, n=1):
        bi.then_inc(self.h, n)
        self.v += n
        return bi


def build_program():
    nc = bass.Bass("TRN2", target_bir_lowering=False,
                   dynamic_dma_scratch_size=8192)

    def din(name, shape, dt):
        return nc.dram_tensor(name, list(shape), dt, kind="ExternalInput")

    xfT = din("xfT", (H, S), F32)
    xoT = din("xoT", (H, R), F32)
    wqP = din("wqP", (H, 4096), F32)
    woP = din("woP", (H, 4096), F32)
    wdkP = din("wdkP", (128, NT * C), F32)
    wkrP = din("wkrP", (128, NT * RD), F32)
    wupkP = din("wupkP", (128, 4 * C), F32)
    wupvP = din("wupvP", (128, 4 * 1024), F32)
    qguP = din("qguP", (GUB * 128, 16384), U8)
    sclguP = din("sclguP", (GUB, 4 * NT * 256), F32)
    qdP = din("qdP", (DB * 128, IFT * 256), U8)
    scldP = din("scldP", (DB, 2 * IFT * 256), F32)
    ckK = din("ckK", (128, S), BF16)
    skK = din("skK", (128, S), BF16)
    cqQ = din("cqQ", (128, R), BF16)
    sqQ = din("sqQ", (128, R), BF16)
    qposP = din("qposP", (1, R), F32)
    kcolP = din("kcolP", (128, NT), F32)

    outT = nc.dram_tensor("outT", [H, R], F32, kind="ExternalOutput")

    def scr(name, shape, dt):
        kind = "ExternalOutput" if name in _DEBUG else "Internal"
        return nc.dram_tensor(name, list(shape), dt, kind=kind)

    ckvS = scr("ckvS", (C, S), BF16)
    krS = scr("krS", (RD, S), BF16)
    ktS = scr("ktS", (KVH * 128, S), BF16)
    vS = scr("vS", (KVH * S, HD), BF16)
    qtS = scr("qtS", (4096, R), BF16)
    atS = scr("atS", (4096, R), BF16)
    stS = scr("stS", (H, R), BF16)
    hS = scr("hS", (I, R), BF16)
    wdS = scr("wdS", (NT * 128, I), BF16)
    sclgBF = scr("sclgBF", (GUB, NT * 1024), BF16)
    scldBF = scr("scldBF", (DB, 2 * IFT * 256), BF16)
    xnS = scr("xnS", (H, R), BF16)
    xnbS = scr("xnbS", (H, S), BF16)
    statS = scr("statS", (16, 1024), F32)
    wgS = scr("wgS", (H, I), BF16)
    wuS = scr("wuS", (H, I), BF16)
    biasS = scr("biasS", (128, 128), F32)

    ctx = ExitStack()
    ld = Sem(ctx, nc, "ld")
    stA = Sem(ctx, nc, "stA")
    stB = Sem(ctx, nc, "stB")
    stC = Sem(ctx, nc, "stC")
    stD = Sem(ctx, nc, "stD")
    pe = Sem(ctx, nc, "pe")
    act = Sem(ctx, nc, "act")
    dve = Sem(ctx, nc, "dve")
    lsc = [Sem(ctx, nc, f"lsc{i}") for i in range(2)]
    lqg = [Sem(ctx, nc, f"lqg{i}") for i in range(2)]
    lw = [Sem(ctx, nc, f"lw{i}") for i in range(2)]
    stq = [Sem(ctx, nc, f"stq{i}") for i in range(2)]
    stw = [Sem(ctx, nc, f"stw{i}") for i in range(2)]
    stv = [Sem(ctx, nc, f"stv{i}") for i in range(2)]

    def _finish():
        phase_end()
        with nc.Block("fin_dev") as blk:
            @blk.sync
            def _(sp):
                for s_ in [stA, stB, stC, stD, ld] + lsc + lqg + lw + stq + stw + stv:
                    sp.wait_ge(s_.h, s_.v)
        return nc

    def sb(name, shape, dt, stack):
        return stack.enter_context(nc.sbuf_tensor(name, list(shape), dt))

    def psf(name, shape, stack, dt=F32):
        return stack.enter_context(nc.psum_tensor(name, list(shape), dt))

    def load_bf16_via_f32(blk, dst, src_dram, stage, rows=slice(0, 128)):
        n = dst.shape[-1]
        half = stage.shape[-1]
        nchunks = (n + half - 1) // half
        for i in range(nchunks):
            c0, c1 = i * half, min((i + 1) * half, n)

            @blk.sync
            def _(sp, c0=c0, c1=c1):
                sp.wait_ge(dve.h, dve.v)
                ld.inc(sp.dma_start(stage[:, 0:c1 - c0],
                                    src_dram[rows, c0:c1]), 16)

            @blk.vector
            def _(v, c0=c0, c1=c1):
                v.wait_ge(ld.h, ld.v)
                dve.inc(v.tensor_copy(dst[:, c0:c1], stage[:, 0:c1 - c0]))

    g = ExitStack()
    ones_col = sb("ones_col", (128, 1), BF16, g)
    ones_row = sb("ones_row", (1, 128), BF16, g)
    ones_row_f = sb("ones_row_f", (1, 128), F32, g)
    ones128 = sb("ones128", (128, 128), BF16, g)
    norms = sb("norms", (1, 512), F32, g)
    bias_all = sb("bias_all", (128, 128), F32, g)
    cst = sb("cst", (1, 4), F32, g)
    cst128 = sb("cst128", (128, 1), F32, g)

    scope_state = [None, None]

    def phase(name):
        if scope_state[0] is not None:
            nc.leave_named_scope(scope_state[0], scope_state[1], False)
        sid, _ = nc.enter_named_scope(name, False)
        scope_state[0], scope_state[1] = name, sid

    def phase_end():
        if scope_state[0] is not None:
            nc.leave_named_scope(scope_state[0], scope_state[1], False)
            scope_state[0] = None

    with nc.Block("init") as blk:
        @blk.vector
        def _(v):
            v.memset(ones_col[:, :], 1.0)
            v.memset(ones_row[:, :], 1.0)
            v.memset(ones_row_f[:, :], 1.0)
            v.memset(ones128[:, :], 1.0)
            v.memset(norms[:, :], 0.0)
            v.memset(cst[0:1, 0:1], LN_EPS)
            v.memset(cst128[:, :], LN_EPS)
            v.memset(cst[0:1, 1:2], math.log(1.02 * INVSQ))

    phase("pscl")
    psc = ExitStack()
    scf32 = sb("scf32", (128, 11008), F32, psc)
    scb16 = sb("scb16", (128, 11008), BF16, psc)

    with nc.Block("pscl") as blk:
        @blk.sync
        def _(sp):
            ld.inc(sp.dma_start(
                scf32[:, :],
                bass.AP(sclguP, 0, [[11008, 128], [1, 11008]])), 16)

        @blk.vector
        def _(v):
            v.wait_ge(ld.h, ld.v)
            dve.inc(v.tensor_copy(scb16[:, :], scf32[:, :]))

        @blk.sync
        def _(sp):
            sp.wait_ge(dve.h, dve.v)
            stC.inc(sp.dma_start(
                bass.AP(sclgBF, 0, [[11008, 128], [1, 11008]]),
                scb16[:, :]), 16)

        @blk.sync
        def _(sp):
            sp.wait_ge(stC.h, stC.v)
            ld.inc(sp.dma_start(
                scf32[:, 0:5504],
                bass.AP(scldP, 0, [[5504, 128], [1, 5504]])), 16)

        @blk.vector
        def _(v):
            v.wait_ge(ld.h, ld.v)
            dve.inc(v.tensor_copy(scb16[:, 0:5504], scf32[:, 0:5504]))

        @blk.sync
        def _(sp):
            sp.wait_ge(dve.h, dve.v)
            stC.inc(sp.dma_start(
                bass.AP(scldBF, 0, [[5504, 128], [1, 5504]]),
                scb16[:, 0:5504]), 16)

        @blk.sync
        def _(sp):
            sp.wait_ge(stC.h, stC.v)

    psc.close()

    def emit_ln_chunk(blk, src_dram, c0, xf32, xch, sqt, m_b, i_b, ps_A, ps_B,
                      dst_fn):
        @blk.sync
        def _(sp):
            sp.wait_ge(dve.h, dve.v)
            for f in range(NT):
                ld.inc(sp.dma_start(xf32[:, f * CW:(f + 1) * CW],
                                    src_dram[f * 128:(f + 1) * 128, c0:c0 + CW]),
                       16)

        @blk.vector
        def _(v):
            v.wait_ge(ld.h, ld.v)
            for f in range(NT):
                bi = v.tensor_copy(xch[:, f * CW:(f + 1) * CW],
                                   xf32[:, f * CW:(f + 1) * CW])
            dve.inc(bi)

        cvt_dve = dve.v
        pe_b = pe.v
        act_b = act.v

        @blk.scalar
        def _(sc):
            sc.wait_ge(dve.h, cvt_dve)
            sc.wait_ge(pe.h, pe_b)
            for f in range(NT):
                act.inc(sc.activation(sqt[:, f * CW:(f + 1) * CW],
                                      xch[:, f * CW:(f + 1) * CW], AF.Square))

        @blk.tensor
        def _(t):
            t.wait_ge(dve.h, dve.v)
            for f in range(NT):
                t.matmul(ps_A[:, :], ones128[:, :],
                         xch[:, f * CW:(f + 1) * CW],
                         start=(f == 0), stop=(f == NT - 1))
            t.wait_ge(act.h, act_b + NT)
            for f in range(NT):
                pe.inc(t.matmul(ps_B[:, :], ones128[:, :],
                                sqt[:, f * CW:(f + 1) * CW],
                                start=(f == 0), stop=(f == NT - 1)))

        @blk.vector
        def _(v):
            v.wait_ge(pe.h, pe.v)
            dve.inc(v.tensor_scalar(m_b[:, :], ps_A[:, :], 1.0 / H, None,
                                    ALU.mult))
            dve.inc(v.tensor_scalar(i_b[:, :], ps_B[:, :], 1.0 / H, None,
                                    ALU.mult))
            v.wait_ge(dve.h, dve.v)
            dve.inc(v.scalar_tensor_tensor(sqt[:, 0:CW], m_b[:, :], -1.0,
                                           m_b[:, :], ALU.mult, ALU.mult))
            v.wait_ge(dve.h, dve.v)
            dve.inc(v.tensor_tensor(i_b[:, :], i_b[:, :], sqt[:, 0:CW],
                                    ALU.add))

        if "statS" in _DEBUG and src_dram.name == "xfT":
            @blk.sync
            def _(sp):
                sp.wait_ge(dve.h, dve.v)
                stA.inc(sp.dma_start(statS[c0 // CW:c0 // CW + 1, 0:256],
                                     m_b[0:1, :]), 16)
                stA.inc(sp.dma_start(statS[c0 // CW:c0 // CW + 1, 256:512],
                                     i_b[0:1, :]), 16)

        @blk.scalar
        def _(sc):
            sc.wait_ge(dve.h, dve.v)
            sc.wait_ge(stA.h, stA.v)
            act.inc(sc.activation(i_b[:, :], i_b[:, :], AF.Ln,
                                  bias=cst128[:, 0:1], scale=1.0))
            sc.wait_ge(act.h, act.v)
            act.inc(sc.activation(i_b[:, :], i_b[:, :], AF.Exp, scale=-0.5))

        if "statS" in _DEBUG and src_dram.name == "xfT":
            @blk.sync
            def _(sp):
                sp.wait_ge(act.h, act.v)
                stA.inc(sp.dma_start(statS[c0 // CW:c0 // CW + 1, 512:768],
                                     i_b[0:1, :]), 16)

            @blk.vector
            def _(v):
                v.wait_ge(stA.h, stA.v)
                dve.inc(v.tensor_copy(m_b[0:1, 0:1], m_b[0:1, 0:1]))

        @blk.vector
        def _(v):
            v.wait_ge(act.h, act.v)
            for f in range(NT):
                dst = dst_fn(f)
                dve.inc(v.tensor_tensor(dst, xch[:, f * CW:(f + 1) * CW],
                                        m_b[:, :], ALU.subtract))
                v.wait_ge(dve.h, dve.v)
                dve.inc(v.tensor_tensor(dst, dst, i_b[:, :], ALU.mult))

    p1 = ExitStack()
    wdk_sb = sb("wdk_sb", (128, NT * C), BF16, p1)
    wkr_sb = sb("wkr_sb", (128, NT * RD), BF16, p1)
    ck_sb = sb("ck_sb", (128, S), BF16, p1)
    sk_sb = sb("sk_sb", (128, S), BF16, p1)
    xf1 = sb("xf1", (128, NT * CW), F32, p1)
    xch1 = sb("xch1", (128, NT * CW), BF16, p1)
    xnch = sb("xnch", (128, NT * CW), BF16, p1)
    sqt1 = sb("sqt1", (128, NT * CW), BF16, p1)
    m_b1 = sb("m_b1", (128, CW), F32, p1)
    i_b1 = sb("i_b1", (128, CW), F32, p1)
    kvev = sb("kvev", (128, 4 * CW), BF16, p1)
    krev = sb("krev", (128, 4 * CW), BF16, p1)
    rp_t = sb("rp_t", (128, CW), BF16, p1)
    ps_A1 = psf("ps_A1", (128, CW), p1)
    ps_B1 = psf("ps_B1", (128, CW), p1)
    ps_kv = [psf(f"ps_kv{m}", (128, CW), p1) for m in range(4)]

    phase("p1")
    with nc.Block("p1_pre") as blk:
        @blk.sync
        def _(sp):
            ld.inc(sp.dma_start(ck_sb[:, :], ckK[:, :]), 16)
            ld.inc(sp.dma_start(sk_sb[:, :], skK[:, :]), 16)
        load_bf16_via_f32(blk, wdk_sb[:, :], wdkP, xf1)
        load_bf16_via_f32(blk, wkr_sb[:, :], wkrP, xf1)

    for ch in range(S // CW):
        c0 = ch * CW
        with nc.Block(f"p1_{ch}") as blk:
            emit_ln_chunk(blk, xfT, c0, xf1, xch1, sqt1, m_b1, i_b1,
                          ps_A1, ps_B1,
                          lambda f: xnch[:, f * CW:(f + 1) * CW])

        if "xnbS" in _DEBUG:
            with nc.Block(f"p1dbg_{ch}") as blk:
                @blk.sync
                def _(sp):
                    sp.wait_ge(dve.h, dve.v)
                    for f in range(NT):
                        stA.inc(sp.dma_start(
                            xnbS[f * 128:(f + 1) * 128, c0:c0 + CW],
                            xnch[:, f * CW:(f + 1) * CW]), 16)

        if _UNTIL == "p1ln":
            continue
        with nc.Block(f"p1kv_{ch}") as blk:
            @blk.tensor
            def _(t):
                t.wait_ge(dve.h, dve.v)
                for m in range(4):
                    for f in range(NT):
                        bi = t.matmul(
                            ps_kv[m][:, :],
                            wdk_sb[:, f * C + m * 128: f * C + (m + 1) * 128],
                            xnch[:, f * CW:(f + 1) * CW],
                            start=(f == 0), stop=(f == NT - 1))
                pe.inc(bi)

            @blk.vector
            def _(v):
                v.wait_ge(pe.h, pe.v)
                v.wait_ge(stA.h, stA.v)
                for m in range(4):
                    bi = v.tensor_copy(kvev[:, m * CW:(m + 1) * CW],
                                       ps_kv[m][:, :])
                dve.inc(bi)

            dve_kv = dve.v

            @blk.tensor
            def _(t):
                t.wait_ge(dve.h, dve_kv)
                for m in range(4):
                    for f in range(NT):
                        bi = t.matmul(
                            ps_kv[m][:, :],
                            wkr_sb[:, f * RD + m * 128: f * RD + (m + 1) * 128],
                            xnch[:, f * CW:(f + 1) * CW],
                            start=(f == 0), stop=(f == NT - 1))
                pe.inc(bi)

            @blk.sync
            def _(sp):
                sp.wait_ge(dve.h, dve_kv)
                for m in range(4):
                    stA.inc(sp.dma_start(
                        ckvS[m * 128:(m + 1) * 128, c0:c0 + CW],
                        kvev[:, m * CW:(m + 1) * CW]), 16)

            @blk.vector
            def _(v):
                v.wait_ge(pe.h, pe.v)
                v.wait_ge(stA.h, stA.v)
                for m in range(4):
                    for s32 in range(4):
                        q0 = s32 * 32
                        p0 = (s32 ^ 1) * 32
                        dst = krev[q0:q0 + 32, m * CW:(m + 1) * CW]
                        v.tensor_tensor(rp_t[q0:q0 + 32, :],
                                        ps_kv[m][p0:p0 + 32, :],
                                        sk_sb[q0:q0 + 32, c0:c0 + CW], ALU.mult)
                        dve.inc(v.tensor_tensor(dst, ps_kv[m][q0:q0 + 32, :],
                                                ck_sb[q0:q0 + 32, c0:c0 + CW],
                                                ALU.mult))
                        v.wait_ge(dve.h, dve.v)
                        dve.inc(v.tensor_tensor(dst, dst, rp_t[q0:q0 + 32, :],
                                                ALU.add))

            @blk.sync
            def _(sp):
                sp.wait_ge(dve.h, dve.v)
                for m in range(4):
                    stA.inc(sp.dma_start(
                        krS[m * 128:(m + 1) * 128, c0:c0 + CW],
                        krev[:, m * CW:(m + 1) * CW]), 16)

    p1.close()
    if _UNTIL in ("p1", "p1ln"):
        return _finish()

    p19 = ExitStack()
    ckv_sb = sb("ckv_sb", (128, 4 * S), BF16, p19)
    kr2_sb = sb("kr2_sb", (128, 4 * S), BF16, p19)
    wupk_sb = sb("wupk_sb", (128, 4 * C), BF16, p19)
    wupv_sb = sb("wupv_sb", (128, 4 * 1024), BF16, p19)
    kass = sb("kass", (128, S), BF16, p19)
    ksq = sb("ksq", (128, 2 * 512), BF16, p19)
    vev = sb("vev", (128, 2 * HD), BF16, p19)
    ps_kn = [psf(f"ps_kn{i}", (128, 512), p19) for i in range(2)]
    ps_v = [psf(f"ps_v{i}", (128, HD), p19) for i in range(2)]
    ps_nr = [psf(f"ps_nr{i}", (128, 512), p19) for i in range(2)]

    phase("p19")
    wst19 = sb("wst19", (128, 4096), F32, p19)
    with nc.Block("p19_pre") as blk:
        load_bf16_via_f32(blk, wupk_sb[:, :], wupkP, wst19)
        load_bf16_via_f32(blk, wupv_sb[:, :], wupvP, wst19)

        @blk.sync
        def _(sp):
            sp.wait_ge(stA.h, stA.v)
            for m in range(4):
                stA.inc(sp.dma_start(ckv_sb[:, m * S:(m + 1) * S],
                                     ckvS[m * 128:(m + 1) * 128, :]), 16)
                stA.inc(sp.dma_start(kr2_sb[:, m * S:(m + 1) * S],
                                     krS[m * 128:(m + 1) * 128, :]), 16)

    for h in range(KVH):
        with nc.Block(f"p19k_{h}") as blk:
            @blk.vector
            def _(v):
                v.wait_ge(stA.h, stA.v)
                v.wait_ge(ld.h, ld.v)
                b0 = 64 * (h % 2)
                hsl = slice((h // 2) * S, (h // 2 + 1) * S)
                v.tensor_copy(kass[0:32, :], kr2_sb[b0:b0 + 32, hsl])
                dve.inc(v.tensor_copy(kass[64:96, :],
                                      kr2_sb[b0 + 32:b0 + 64, hsl]))

            dve_cp = dve.v
            pe_b = pe.v

            @blk.tensor
            def _(t):
                t.wait_ge(stA.h, stA.v)
                t.wait_ge(ld.h, ld.v)
                mo = (h // 2) * 128 + (h % 2) * 64
                for nch in range(8):
                    if nch >= 2:
                        t.wait_ge(dve.h, dve_cp + (nch - 1))
                    for f in range(4):
                        bi = t.matmul(
                            ps_kn[nch % 2][0:64, :],
                            wupk_sb[:, f * C + mo: f * C + mo + 64],
                            ckv_sb[:, f * S + nch * 512: f * S + (nch + 1) * 512],
                            start=(f == 0), stop=(f == 3))
                        pe.inc(bi) if f == 3 else None

            @blk.vector
            def _(v):
                for nch in range(8):
                    v.wait_ge(pe.h, pe_b + nch + 1)
                    sl = slice(nch * 512, (nch + 1) * 512)
                    v.tensor_copy(kass[32:64, sl], ps_kn[nch % 2][0:32, :])
                    dve.inc(v.tensor_copy(kass[96:128, sl],
                                          ps_kn[nch % 2][32:64, :]))

            @blk.sync
            def _(sp):
                sp.wait_ge(dve.h, dve.v)
                stA.inc(sp.dma_start(ktS[h * 128:(h + 1) * 128, :], kass[:, :]),
                        16)

            act_b = act.v
            pe_b2 = pe.v
            dve_b2 = dve.v

            @blk.scalar
            def _(sc):
                sc.wait_ge(dve.h, dve_b2)
                for j in range(8):
                    if j >= 2:
                        sc.wait_ge(pe.h, pe_b2 + j - 1)
                    act.inc(sc.activation(ksq[:, (j % 2) * 512:(j % 2 + 1) * 512],
                                          kass[:, j * 512:(j + 1) * 512],
                                          AF.Square))

            @blk.tensor
            def _(t):
                for j in range(8):
                    t.wait_ge(act.h, act_b + j + 1)
                    if j >= 2:
                        t.wait_ge(dve.h, dve_b2 + j - 1)
                    pe.inc(t.matmul(ps_nr[j % 2][:, :], ones128[:, :],
                                    ksq[:, (j % 2) * 512:(j % 2 + 1) * 512],
                                    start=True, stop=True))

            @blk.vector
            def _(v):
                for j in range(8):
                    v.wait_ge(pe.h, pe_b2 + j + 1)
                    dve.inc(v.tensor_reduce(norms[0:1, 200 + j:201 + j],
                                            ps_nr[j % 2][0:1, :], AX.X, ALU.max))
                v.wait_ge(dve.h, dve.v)
                v.tensor_reduce(norms[0:1, h:h + 1],
                                norms[0:1, 200:208], AX.X, ALU.max)
                dve.inc(v.tensor_copy(norms[0:1, 220:221], norms[0:1, h:h + 1]))

        with nc.Block(f"p19v_{h}") as blk:
            stA_b = stA.v
            pe_b3 = pe.v
            dve_b3 = dve.v

            @blk.tensor
            def _(t):
                for kt in range(NT):
                    if kt >= 2:
                        t.wait_ge(dve.h, dve_b3 + (kt - 1))
                    for f in range(4):
                        bi = t.matmul(
                            ps_v[kt % 2][:, :],
                            ckv_sb[:, f * S + kt * 128: f * S + (kt + 1) * 128],
                            wupv_sb[:, f * 1024 + h * 128: f * 1024 + (h + 1) * 128],
                            start=(f == 0), stop=(f == 3))
                    pe.inc(bi)

            stv_b = [stv[0].v, stv[1].v]

            @blk.vector
            def _(v):
                for kt in range(NT):
                    v.wait_ge(pe.h, pe_b3 + kt + 1)
                    if kt >= 2:
                        v.wait_ge(stv[kt % 2].h, stv_b[kt % 2] + 16 * (kt // 2))
                    dve.inc(v.tensor_copy(
                        vev[:, (kt % 2) * HD:(kt % 2 + 1) * HD],
                        ps_v[kt % 2][:, :]))

            @blk.sync
            def _(sp):
                dve_b4 = dve.v - NT
                for kt in range(NT):
                    sp.wait_ge(dve.h, dve_b4 + kt + 1)
                    stv[kt % 2].inc(sp.dma_start(
                        vS[h * S + kt * 128: h * S + (kt + 1) * 128, :],
                        vev[:, (kt % 2) * HD:(kt % 2 + 1) * HD]), 16)

    p19.close()
    if _UNTIL == "p19":
        return _finish()

    gx = ExitStack()
    xn_own = sb("xn_own", (128, NT * R), BF16, gx)
    p15 = ExitStack()
    p15a = ExitStack()
    xf1b = sb("xf1b", (128, NT * CW), F32, p15a)
    xch2 = sb("xch2", (128, NT * CW), BF16, p15a)
    sqt2 = sb("sqt2", (128, NT * CW), BF16, p15a)
    m_b2 = sb("m_b2", (128, CW), F32, p15a)
    i_b2 = sb("i_b2", (128, CW), F32, p15a)
    ps_A2 = psf("ps_A2", (128, CW), p15a)
    ps_B2 = psf("ps_B2", (128, CW), p15a)

    phase("p15ln")
    for ch in range(R // CW):
        c0 = ch * CW
        with nc.Block(f"p15_{ch}") as blk:
            emit_ln_chunk(blk, xoT, c0, xf1b, xch2, sqt2, m_b2, i_b2,
                          ps_A2, ps_B2,
                          lambda f: xn_own[:, f * R + c0: f * R + c0 + CW])

    if "xnS" in _DEBUG:
        with nc.Block("p15dbg") as blk:
            @blk.sync
            def _(sp):
                sp.wait_ge(dve.h, dve.v)
                for f in range(NT):
                    stA.inc(sp.dma_start(xnS[f * 128:(f + 1) * 128, :],
                                         xn_own[:, f * R:(f + 1) * R]), 16)

    p15a.close()
    wq_sb = [sb(f"wq_sb{i}", (128, H), BF16, p15) for i in range(2)]
    q_ev = [sb(f"q_ev{i}", (128, 512), BF16, p15) for i in range(2)]
    q_t1 = sb("q_t1", (128, 512), BF16, p15)
    qsq2 = sb("qsq2", (128, 512), BF16, p15)
    cq_sb = sb("cq_sb", (128, R), BF16, p15)
    sq_sb = sb("sq_sb", (128, R), BF16, p15)
    ps_q = [psf(f"ps_q{i}", (128, 512), p15) for i in range(2)]
    ps_qn = [psf(f"ps_qn{i}", (128, 512), p15) for i in range(2)]

    phase("p15q")
    wstq = [sb(f"wstq{i}", (128, H), F32, p15) for i in range(2)]
    with nc.Block("p15q_pre") as blk:
        @blk.sync
        def _(sp):
            ld.inc(sp.dma_start(cq_sb[:, :], cqQ[:, :]), 16)
            ld.inc(sp.dma_start(sq_sb[:, :], sqQ[:, :]), 16)
            ld.inc(sp.dma_start(wstq[0][:, :], wqP[0:128, :]), 16)

        @blk.vector
        def _(v):
            v.wait_ge(ld.h, ld.v)
            dve.inc(v.tensor_copy(wq_sb[0][:, :], wstq[0][:, :]))

    wq_cvt = {0: dve.v, 1: 0}

    for m in range(NT):
        with nc.Block(f"p15q_{m}") as blk:
            @blk.sync
            def _(sp):
                if m + 1 < NT:
                    sp.wait_ge(dve.h, dve.v)
                    ld.inc(sp.dma_start(wstq[(m + 1) % 2][:, :],
                                        wqP[(m + 1) * 128:(m + 2) * 128, :]), 16)

            if m + 1 < NT:
                @blk.vector
                def _(v):
                    v.wait_ge(ld.h, ld.v)
                    v.wait_ge(pe.h, pe.v)
                    dve.inc(v.tensor_copy(wq_sb[(m + 1) % 2][:, :],
                                          wstq[(m + 1) % 2][:, :]))
                wq_cvt[(m + 1) % 2] = dve.v
            pe_b = pe.v
            dve_b = dve.v
            act_b = act.v
            stq_b = [stq[0].v, stq[1].v]

            @blk.tensor
            def _(t):
                t.wait_ge(dve.h, wq_cvt[m % 2])
                for cch in range(4):
                    if cch >= 2:
                        t.wait_ge(dve.h, dve_b + 3 * (cch - 1))
                    for f in range(NT):
                        bi = t.matmul(
                            ps_q[cch % 2][:, :],
                            wq_sb[m % 2][:, f * 128:(f + 1) * 128],
                            xn_own[:, f * R + cch * 512: f * R + cch * 512 + 512],
                            start=(f == 0), stop=(f == NT - 1))
                    pe.inc(bi)

            @blk.vector
            def _(v):
                for cch in range(4):
                    v.wait_ge(pe.h, pe_b + cch + 1)
                    if cch >= 2:
                        v.wait_ge(stq[cch % 2].h, stq_b[cch % 2] + 16)
                    pq = ps_q[cch % 2]
                    qe = q_ev[cch % 2]
                    rsl = slice(cch * 512, cch * 512 + 512)
                    for s64 in range(2):
                        a0, b0 = s64 * 64, (1 - s64) * 64
                        v.tensor_tensor(q_t1[a0:a0 + 64, :], pq[b0:b0 + 64, :],
                                        sq_sb[a0:a0 + 64, rsl], ALU.mult)
                        dve.inc(v.tensor_tensor(qe[a0:a0 + 64, :],
                                                pq[a0:a0 + 64, :],
                                                cq_sb[a0:a0 + 64, rsl],
                                                ALU.mult))
                        v.wait_ge(dve.h, dve.v)
                        if s64 == 0:
                            v.tensor_tensor(qe[a0:a0 + 64, :], qe[a0:a0 + 64, :],
                                            q_t1[a0:a0 + 64, :], ALU.add)
                        else:
                            dve.inc(v.tensor_tensor(qe[a0:a0 + 64, :],
                                                    qe[a0:a0 + 64, :],
                                                    q_t1[a0:a0 + 64, :],
                                                    ALU.add))

            @blk.sync
            def _(sp):
                for cch in range(4):
                    sp.wait_ge(dve.h, dve_b + 3 * (cch + 1))
                    stq[cch % 2].inc(sp.dma_start(
                        qtS[m * 128:(m + 1) * 128, cch * 512:cch * 512 + 512],
                        q_ev[cch % 2][:, :]), 16)

            @blk.scalar
            def _(sc):
                for cch in range(4):
                    sc.wait_ge(dve.h, dve_b + 3 * (cch + 1))
                    if cch >= 1:
                        sc.wait_ge(pe.h, pe_b + 4 + cch)
                    act.inc(sc.activation(qsq2[:, :], q_ev[cch % 2][:, :],
                                          AF.Square))

            @blk.tensor
            def _(t):
                qn_dve_b = dve_b + 12
                for cch in range(4):
                    t.wait_ge(act.h, act_b + cch + 1)
                    if cch >= 2:
                        t.wait_ge(dve.h, qn_dve_b + cch - 1)
                    pe.inc(t.matmul(ps_qn[cch % 2][:, :], ones128[:, :],
                                    qsq2[:, :], start=True, stop=True))

            @blk.vector
            def _(v):
                pe_c = pe_b + 4
                for cch in range(4):
                    v.wait_ge(pe.h, pe_c + cch + 1)
                    dve.inc(v.tensor_reduce(
                        norms[0:1, 8 + m * 4 + cch: 9 + m * 4 + cch],
                        ps_qn[cch % 2][0:1, :], AX.X, ALU.max))


    phase("pbias")
    ps_bb = psf("ps_bb", (128, 128), p15)
    with nc.Block("pbias") as blk:
        @blk.scalar
        def _(sc):
            sc.wait_ge(dve.h, dve.v)
            sc.activation(norms[0:1, 136:264], norms[0:1, 8:136], AF.Ln)
            act.inc(sc.activation(norms[0:1, 300:308], norms[0:1, 0:8], AF.Ln))

        @blk.vector
        def _(v):
            v.wait_ge(act.h, act.v)
            for h in range(8):
                bi = v.tensor_scalar(norms[0:1, 136 + 16 * h:136 + 16 * (h + 1)],
                                     norms[0:1, 136 + 16 * h:136 + 16 * (h + 1)],
                                     norms[0:1, 300 + h:301 + h], None, ALU.add)
            dve.inc(bi)

        @blk.scalar
        def _(sc):
            sc.wait_ge(dve.h, dve.v)
            act.inc(sc.activation(norms[0:1, 136:264], norms[0:1, 136:264],
                                  AF.Exp, scale=0.5, bias=cst[0:1, 1:2]))

        @blk.vector
        def _(v):
            v.wait_ge(act.h, act.v)
            dve.inc(v.tensor_scalar(norms[0:1, 136:264], norms[0:1, 136:264],
                                    -1.0, None, ALU.mult))

        @blk.tensor
        def _(t):
            t.wait_ge(dve.h, dve.v)
            pe.inc(t.matmul(ps_bb[:, :], ones_row_f[:, :], norms[0:1, 136:264],
                            start=True, stop=True))

        @blk.vector
        def _(v):
            v.wait_ge(pe.h, pe.v)
            dve.inc(v.tensor_copy(bias_all[:, :], ps_bb[:, :]))

        if "biasS" in _DEBUG:
            @blk.sync
            def _(sp):
                sp.wait_ge(dve.h, dve.v)
                stA.inc(sp.dma_start(biasS[:, :], bias_all[:, :]), 16)

    p15.close()
    if _UNTIL == "p15":
        return _finish()

    p3a = ExitStack()
    qgu_sb = [sb(f"qgu_sb{i}", (128, 16384), U8, p3a) for i in range(2)]
    scl_bf = [sb(f"scl_bf{i}", (128, 1024), BF16, p3a) for i in range(2)]
    wg_sb = sb("wg_sb", (128, NT * 256), BF16, p3a)
    wu_sb = sb("wu_sb", (128, NT * 256), BF16, p3a)
    dq_t = sb("dq_t", (128, 256), BF16, p3a)
    sg_tmp = sb("sg_tmp", (128, R), BF16, p3a)
    h_ev = sb("h_ev", (128, R), BF16, p3a)
    ps_gu = psf("ps_gu", (128, R), p3a)

    dq_done = [0, 0]

    phase("p3a")
    with nc.Block("p3a_pre") as blk:
        @blk.sync
        def _(sp):
            lqg[0].inc(sp.dma_start(qgu_sb[0][:, :], qguP[0:128, :]), 16)

    for b in range(GUB):
        bi_ = b % 2
        with nc.Block(f"p3a_{b}") as blk:
            dve_b = dve.v
            scl_at = []

            scl_at = []

            @blk.scalar
            def _(sc):
                for gi in range(NT):
                    if gi >= 2:
                        sc.wait_ge(dve.h, dve_b + 4 * (gi - 1))
                    lsc[gi % 2].inc(sc.dma_start(
                        scl_bf[gi % 2][:, :],
                        bass.AP(sclgBF, b * (NT * 1024) + gi * 1024,
                                [[0, 128], [1, 1024]])), 16)
                    scl_at.append(lsc[gi % 2].v)

            @blk.sync
            def _(sp):
                if b + 1 < GUB:
                    sp.wait_ge(dve.h, dq_done[(b + 1) % 2])
                    lqg[(b + 1) % 2].inc(sp.dma_start(
                        qgu_sb[(b + 1) % 2][:, :],
                        qguP[(b + 1) * 128:(b + 2) * 128, :]), 16)

            pe_prev = pe.v

            @blk.vector
            def _(v):
                v.wait_ge(pe.h, pe_prev)
                v.wait_ge(lqg[bi_].h, lqg[bi_].v)
                for f in range(NT):
                    v.wait_ge(lsc[f % 2].h, scl_at[f])
                    sc_ = scl_bf[f % 2]
                    qg_ = qgu_sb[bi_][:, f * 512:f * 512 + 256]
                    dve.inc(v.tensor_tensor(dq_t[:, :], qg_, sc_[:, 0:256],
                                            ALU.subtract))
                    v.wait_ge(dve.h, dve.v)
                    dve.inc(v.tensor_tensor(wg_sb[:, f * 256:(f + 1) * 256],
                                            dq_t[:, :], sc_[:, 256:512],
                                            ALU.mult))
                    qu_ = qgu_sb[bi_][:, f * 512 + 256:f * 512 + 512]
                    dve.inc(v.tensor_tensor(dq_t[:, :], qu_, sc_[:, 512:768],
                                            ALU.subtract))
                    v.wait_ge(dve.h, dve.v)
                    dve.inc(v.tensor_tensor(wu_sb[:, f * 256:(f + 1) * 256],
                                            dq_t[:, :], sc_[:, 768:1024],
                                            ALU.mult))

            dq_done[bi_] = dve.v
            dq_v = dve.v

            for gidx, (mat, wsrc, msub) in enumerate(
                    ((0, wg_sb, 0), (1, wu_sb, 0), (0, wg_sb, 1), (1, wu_sb, 1))):

                @blk.tensor
                def _(t, gidx=gidx, mat=mat, wsrc=wsrc, msub=msub):
                    if mat == 0:
                        t.wait_ge(dve.h, dve.v)
                    else:
                        t.wait_ge(act.h, act.v)
                    for f in range(NT):
                        if gidx == 0:
                            t.wait_ge(dve.h, dve_b + 4 * f + 2)
                        elif gidx == 1:
                            t.wait_ge(dve.h, dve_b + 4 * f + 4)
                        for nch in range(4):
                            bi2 = t.matmul(
                                ps_gu[:, nch * 512:(nch + 1) * 512],
                                wsrc[:, f * 256 + msub * 128:
                                     f * 256 + msub * 128 + 128],
                                xn_own[:, f * R + nch * 512:
                                       f * R + (nch + 1) * 512],
                                start=(f == 0), stop=(f == NT - 1))
                    pe.inc(bi2)

                if mat == 0:
                    @blk.scalar
                    def _(sc):
                        sc.wait_ge(pe.h, pe.v)
                        act.inc(sc.activation(sg_tmp[:, :], ps_gu[:, :],
                                              AF.Silu))
                else:
                    @blk.vector
                    def _(v):
                        v.wait_ge(pe.h, pe.v)
                        v.wait_ge(stB.h, stB.v)
                        dve.inc(v.tensor_tensor(h_ev[:, :], sg_tmp[:, :],
                                                ps_gu[:, :], ALU.mult))

                    @blk.sync
                    def _(sp, msub=msub):
                        sp.wait_ge(dve.h, dve.v)
                        irow = b * 256 + msub * 128
                        stB.inc(sp.dma_start(hS[irow:irow + 128, :],
                                             h_ev[:, :]), 16)

    p3a.close()
    gx.close()
    if _UNTIL == "p3a":
        return _finish()

    p2 = ExitStack()
    kt_sb = [sb(f"kt_sb{i}", (128, S), BF16, p2) for i in range(2)]
    vh_sb = [sb(f"vh_sb{i}", (128, S), BF16, p2) for i in range(2)]
    qt_sb = [sb(f"qt_sb{i}", (128, 512), BF16, p2) for i in range(2)]
    pt_sb = [sb(f"pt_sb{i}", (128, 512), BF16, p2) for i in range(2)]
    mask_sb = sb("mask_sb", (128, 8 * 512), BF16, p2)
    at_sb = sb("at_sb", (128, NH * 512), BF16, p2)
    il_sb = sb("il_sb", (128, 512), F32, p2)
    qpos_sb = sb("qpos_sb", (1, R), F32, p2)
    kcol_sb = sb("kcol_sb", (128, NT), F32, p2)
    ps_st = [psf(f"ps_st{i}", (128, 512), p2) for i in range(3)]
    ps_o = psf("ps_o", (128, 512), p2)
    ps_l = psf("ps_l", (128, 512), p2)
    ps_x = psf("ps_x", (128, 512), p2)

    phase("p2")
    with nc.Block("p2_pre") as blk:
        @blk.sync
        def _(sp):
            ld.inc(sp.dma_start(qpos_sb[:, :], qposP[:, :]), 16)
            ld.inc(sp.dma_start(kcol_sb[:, :], kcolP[:, :]), 16)

    for cch in range(4):
        L = LIM[cch]
        m0 = M0[cch]
        with nc.Block(f"p2m_{cch}") as blk:
            @blk.tensor
            def _(t):
                t.wait_ge(ld.h, ld.v)
                t.wait_ge(dve.h, dve.v)
                pe.inc(t.matmul(ps_x[:, :], ones_row_f[:, :],
                                qpos_sb[0:1, cch * 512:(cch + 1) * 512],
                                start=True, stop=True))

            @blk.vector
            def _(v):
                v.wait_ge(pe.h, pe.v)
                v.wait_ge(stB.h, stB.v)
                for kt in range(m0, L):
                    bi = v.tensor_scalar(
                        mask_sb[:, (kt - m0) * 512:(kt - m0 + 1) * 512],
                        ps_x[:, :], kcol_sb[:, kt:kt + 1], None, ALU.is_ge)
                dve.inc(bi)

        for kvh in range(KVH):
            kbi = kvh % 2
            with nc.Block(f"p2kv_{cch}_{kvh}") as blk:
                @blk.sync
                def _(sp):
                    sp.wait_ge(pe.h, pe.v)
                    stB.inc(sp.dma_start(
                        kt_sb[kbi][:, 0:L * 128],
                        ktS[kvh * 128:(kvh + 1) * 128, 0:L * 128]), 16)
                    for kt in range(L):
                        stB.inc(sp.dma_start(
                            vh_sb[kbi][:, kt * 128:(kt + 1) * 128],
                            vS[kvh * S + kt * 128: kvh * S + (kt + 1) * 128, :]),
                            16)

            for qh in range(4):
                head = kvh * 4 + qh
                qbi = head % 2
                slot = head * 4 + cch
                with nc.Block(f"p2h_{cch}_{head}") as blk:
                    @blk.sync
                    def _(sp):
                        sp.wait_ge(pe.h, pe.v)
                        stB.inc(sp.dma_start(
                            qt_sb[qbi][:, :],
                            qtS[head * 128:(head + 1) * 128,
                                cch * 512:(cch + 1) * 512]), 16)

                    pe_b = pe.v
                    act_b = act.v
                    dve_b = dve.v

                    def st_done(kt):
                        return pe_b + (1 if kt == 0 else 3 * kt - 1)

                    def pv_done(kt):
                        return pe_b + 3 * (kt + 1)

                    @blk.tensor
                    def _(t):
                        t.wait_ge(stB.h, stB.v)
                        t.wait_ge(dve.h, dve_b)
                        for it in range(L + 1):
                            if it < L:
                                kt = it
                                if kt >= 3:
                                    t.wait_ge(act.h, act_b + kt - 2)
                                pe.inc(t.matmul(
                                    ps_st[kt % 3][:, :],
                                    kt_sb[kbi][:, kt * 128:(kt + 1) * 128],
                                    qt_sb[qbi][:, :], start=True, stop=True))
                            if it >= 1:
                                kt = it - 1
                                if kt < m0:
                                    t.wait_ge(act.h, act_b + kt + 1)
                                else:
                                    t.wait_ge(dve.h, dve_b + (kt - m0) + 1)
                                pe.inc(t.matmul(
                                    ps_o[:, :],
                                    vh_sb[kbi][:, kt * 128:(kt + 1) * 128],
                                    pt_sb[kt % 2][:, :],
                                    start=(kt == 0), stop=(kt == L - 1)))
                                pe.inc(t.matmul(
                                    ps_l[:, :], ones128[:, :],
                                    pt_sb[kt % 2][:, :],
                                    start=(kt == 0), stop=(kt == L - 1)))

                    @blk.scalar
                    def _(sc):
                        for kt in range(L):
                            sc.wait_ge(pe.h, st_done(kt))
                            if kt >= 2:
                                sc.wait_ge(pe.h, pv_done(kt - 2))
                            act.inc(sc.activation(
                                pt_sb[kt % 2][:, :], ps_st[kt % 3][:, :],
                                AF.Exp, bias=bias_all[:, slot:slot + 1],
                                scale=INVSQ))

                    @blk.vector
                    def _(v):
                        for kt in range(m0, L):
                            v.wait_ge(act.h, act_b + kt + 1)
                            dve.inc(v.tensor_tensor(
                                pt_sb[kt % 2][:, :], pt_sb[kt % 2][:, :],
                                mask_sb[:, (kt - m0) * 512:(kt - m0 + 1) * 512],
                                ALU.mult))

                    pe_tail = pe_b + 3 * L

                    @blk.vector
                    def _(v):
                        v.wait_ge(pe.h, pe_tail)
                        dve.inc(v.reciprocal(il_sb[:, :], ps_l[:, :]))
                        v.wait_ge(dve.h, dve.v)
                        dve.inc(v.tensor_tensor(
                            at_sb[:, head * 512:(head + 1) * 512], ps_o[:, :],
                            il_sb[:, :], ALU.mult))

        with nc.Block(f"p2st_{cch}") as blk:
            @blk.sync
            def _(sp):
                sp.wait_ge(dve.h, dve.v)
                for hh in range(NH):
                    stB.inc(sp.dma_start(
                        atS[hh * 128:(hh + 1) * 128, cch * 512:(cch + 1) * 512],
                        at_sb[:, hh * 512:(hh + 1) * 512]), 16)

    p2.close()
    if _UNTIL == "p2":
        return _finish()

    p25 = ExitStack()
    at2_sb = sb("at2_sb", (128, NH * R), BF16, p25)
    wo_sb = [sb(f"wo_sb{i}", (128, H), BF16, p25) for i in range(2)]
    wsto = [sb(f"wsto{i}", (128, H), F32, p25) for i in range(2)]
    st_ev = [sb(f"st_ev{i}", (128, 512), BF16, p25) for i in range(2)]
    ps_w = [psf(f"ps_w{i}", (128, 512), p25) for i in range(2)]

    phase("p25")
    with nc.Block("p25_pre") as blk:
        @blk.sync
        def _(sp):
            sp.wait_ge(stB.h, stB.v)
            for hh in range(NH):
                stC.inc(sp.dma_start(at2_sb[:, hh * R:(hh + 1) * R],
                                     atS[hh * 128:(hh + 1) * 128, :]), 16)

        @blk.scalar
        def _(sc):
            ld.inc(sc.dma_start(wsto[0][:, :], woP[0:128, :]), 16)

        @blk.vector
        def _(v):
            v.wait_ge(ld.h, ld.v)
            dve.inc(v.tensor_copy(wo_sb[0][:, :], wsto[0][:, :]))

    wo_cvt = {0: dve.v, 1: 0}

    for m in range(NT):
        with nc.Block(f"p25_{m}") as blk:
            @blk.scalar
            def _(sc):
                if m + 1 < NT:
                    sc.wait_ge(dve.h, dve.v)
                    ld.inc(sc.dma_start(wsto[(m + 1) % 2][:, :],
                                        woP[(m + 1) * 128:(m + 2) * 128, :]), 16)

            if m + 1 < NT:
                @blk.vector
                def _(v):
                    v.wait_ge(ld.h, ld.v)
                    v.wait_ge(pe.h, pe.v)
                    dve.inc(v.tensor_copy(wo_sb[(m + 1) % 2][:, :],
                                          wsto[(m + 1) % 2][:, :]))
                wo_cvt[(m + 1) % 2] = dve.v

            pe_b = pe.v
            dve_b = dve.v
            stC_b = stC.v

            @blk.tensor
            def _(t):
                t.wait_ge(dve.h, wo_cvt[m % 2])
                t.wait_ge(stC.h, stC_b)
                for cch in range(4):
                    if cch >= 2:
                        t.wait_ge(dve.h, dve_b + cch - 1)
                    for f in range(NH):
                        bi = t.matmul(
                            ps_w[cch % 2][:, :],
                            wo_sb[m % 2][:, f * 128:(f + 1) * 128],
                            at2_sb[:, f * R + cch * 512: f * R + cch * 512 + 512],
                            start=(f == 0), stop=(f == NH - 1))
                    pe.inc(bi)

            stw_b = [stw[0].v, stw[1].v]

            @blk.vector
            def _(v):
                for cch in range(4):
                    v.wait_ge(pe.h, pe_b + cch + 1)
                    if cch >= 2:
                        v.wait_ge(stw[cch % 2].h, stw_b[cch % 2] + 16)
                    dve.inc(v.tensor_copy(st_ev[cch % 2][:, :],
                                          ps_w[cch % 2][:, :]))

            @blk.sync
            def _(sp):
                for cch in range(4):
                    sp.wait_ge(dve.h, dve_b + cch + 1)
                    stw[cch % 2].inc(sp.dma_start(
                        stS[m * 128:(m + 1) * 128, cch * 512:(cch + 1) * 512],
                        st_ev[cch % 2][:, :]), 16)

    p25.close()
    if _UNTIL == "p25":
        return _finish()

    p3d = ExitStack()
    GD = 4
    NGD = (IFT + GD - 1) // GD
    qd_sb = [sb(f"qd_sb{i}", (128, IFT * 256), U8, p3d) for i in range(2)]
    scd_bf = [sb(f"scd_bf{i}", (128, 512), BF16, p3d) for i in range(2)]
    wd_ev = sb("wd_ev", (128, IFT * 256), BF16, p3d)
    dq_t2 = sb("dq_t2", (128, 256), BF16, p3d)

    dqd_done = [0, 0]
    phase("p3d")
    with nc.Block("p3d_pre") as blk:
        @blk.sync
        def _(sp):
            lqg[0].inc(sp.dma_start(qd_sb[0][:, :], qdP[0:128, :]), 16)

    for b in range(DB):
        bi_ = b % 2
        with nc.Block(f"p3d_{b}") as blk:
            dve_b = dve.v
            scl_at = []

            scl_at = []

            @blk.scalar
            def _(sc):
                for f in range(IFT):
                    if f >= 2:
                        sc.wait_ge(dve.h, dve_b + 2 * (f - 1))
                    lsc[f % 2].inc(sc.dma_start(
                        scd_bf[f % 2][:, :],
                        bass.AP(scldBF, b * (2 * IFT * 256) + f * 512,
                                [[0, 128], [1, 512]])), 16)
                    scl_at.append(lsc[f % 2].v)

            @blk.sync
            def _(sp):
                if b + 1 < DB:
                    sp.wait_ge(dve.h, dqd_done[(b + 1) % 2])
                    lqg[(b + 1) % 2].inc(sp.dma_start(
                        qd_sb[(b + 1) % 2][:, :],
                        qdP[(b + 1) * 128:(b + 2) * 128, :]), 16)

            stC_b = stC.v

            @blk.vector
            def _(v):
                v.wait_ge(stC.h, stC_b)
                v.wait_ge(lqg[bi_].h, lqg[bi_].v)
                for f in range(IFT):
                    v.wait_ge(lsc[f % 2].h, scl_at[f])
                    sc_ = scd_bf[f % 2]
                    qsl = qd_sb[bi_][:, f * 256:(f + 1) * 256]
                    dve.inc(v.tensor_tensor(dq_t2[:, :], qsl, sc_[:, 0:256],
                                            ALU.subtract))
                    v.wait_ge(dve.h, dve.v)
                    dve.inc(v.tensor_tensor(wd_ev[:, f * 256:(f + 1) * 256],
                                            dq_t2[:, :], sc_[:, 256:512],
                                            ALU.mult))

            dqd_done[bi_] = dve.v
            dve_b2 = dve.v

            @blk.sync
            def _(sp):
                sp.wait_ge(dve.h, dve_b2)
                for msub in range(2):
                    m = b * 2 + msub
                    for f0 in range(0, IFT, 11):
                        f1 = min(f0 + 11, IFT)
                        src = bass.AP(wd_ev, msub * 128 + f0 * 256,
                                      [[IFT * 256, 128], [256, f1 - f0], [1, 128]])
                        stC.inc(sp.dma_start(
                            wdS[m * 128:(m + 1) * 128, f0 * 128:f1 * 128], src),
                            16)

    p3d.close()
    if _UNTIL == "p3d":
        return _finish()

    p3b = ExitStack()
    h_sb = sb("h_sb", (128, IFT * 512), BF16, p3b)
    wd_sb = [sb(f"wd_sb{i}", (128, I), BF16, p3b) for i in range(2)]
    st_in = [sb(f"st_in{i}", (128, 512), BF16, p3b) for i in range(2)]
    x_in = [sb(f"x_in{i}", (128, 512), F32, p3b) for i in range(2)]
    gf_t = sb("gf_t", (128, 512), BF16, p3b)
    o_ev = [sb(f"o_ev{i}", (128, 512), F32, p3b) for i in range(2)]
    ps_d = [psf(f"ps_d{i}", (128, 512), p3b) for i in range(2)]

    phase("p3b")
    for cch in range(4):
        with nc.Block(f"p3b_{cch}") as blk:
            @blk.sync
            def _(sp):
                sp.wait_ge(pe.h, pe.v)
                for f in range(IFT):
                    stD.inc(sp.dma_start(
                        h_sb[:, f * 512:(f + 1) * 512],
                        hS[f * 128:(f + 1) * 128, cch * 512:(cch + 1) * 512]),
                        16)

            @blk.scalar
            def _(sc):
                sc.wait_ge(pe.h, pe.v)
                lw[0].inc(sc.dma_start(wd_sb[0][:, :], wdS[0:128, :]), 16)
                lw[0].inc(sc.dma_start(st_in[0][:, :],
                                       stS[0:128, cch * 512:(cch + 1) * 512]), 16)
                lw[0].inc(sc.dma_start(x_in[0][:, :],
                                       xoT[0:128, cch * 512:(cch + 1) * 512]), 16)

        for m in range(NT):
            with nc.Block(f"p3b_{cch}_{m}") as blk:
                @blk.scalar
                def _(sc):
                    if m + 1 < NT:
                        sc.wait_ge(pe.h, pe.v)
                        par = (m + 1) % 2
                        lw[par].inc(sc.dma_start(
                            wd_sb[par][:, :],
                            wdS[(m + 1) * 128:(m + 2) * 128, :]), 16)
                        lw[par].inc(sc.dma_start(
                            st_in[par][:, :],
                            stS[(m + 1) * 128:(m + 2) * 128,
                                cch * 512:(cch + 1) * 512]), 16)
                        lw[par].inc(sc.dma_start(
                            x_in[par][:, :],
                            xoT[(m + 1) * 128:(m + 2) * 128,
                                cch * 512:(cch + 1) * 512]), 16)

                lw_need = lw[m % 2].v

                @blk.tensor
                def _(t):
                    t.wait_ge(lw[m % 2].h, lw_need)
                    t.wait_ge(stD.h, stD.v)
                    t.wait_ge(act.h, act.v)
                    for f in range(IFT):
                        bi = t.matmul(ps_d[m % 2][:, :],
                                      wd_sb[m % 2][:, f * 128:(f + 1) * 128],
                                      h_sb[:, f * 512:(f + 1) * 512],
                                      start=(f == 0), stop=(f == IFT - 1))
                    pe.inc(bi)

                @blk.scalar
                def _(sc):
                    sc.wait_ge(pe.h, pe.v)
                    sc.wait_ge(dve.h, dve.v)
                    act.inc(sc.activation(gf_t[:, :], ps_d[m % 2][:, :],
                                          AF.Silu))

                @blk.vector
                def _(v):
                    v.wait_ge(act.h, act.v)
                    v.wait_ge(stD.h, stD.v)
                    v.tensor_tensor(gf_t[:, :], st_in[m % 2][:, :], gf_t[:, :],
                                    ALU.mult)
                    dve.inc(v.tensor_tensor(o_ev[m % 2][:, :], x_in[m % 2][:, :],
                                            gf_t[:, :], ALU.add))

                @blk.sync
                def _(sp):
                    sp.wait_ge(dve.h, dve.v)
                    stD.inc(sp.dma_start(
                        outT[m * 128:(m + 1) * 128, cch * 512:(cch + 1) * 512],
                        o_ev[m % 2][:, :]), 16)

    phase_end()
    with nc.Block("fin") as blk:
        @blk.sync
        def _(sp):
            sp.wait_ge(stD.h, stD.v)
            sp.wait_ge(ld.h, ld.v)
            for s_ in lsc + lqg + lw + stq + stw + stv:
                sp.wait_ge(s_.h, s_.v)

    p3b.close()
    g.close()
    ctx.close()
    return nc


def _prep_core_inputs(core, x, w):
    b = core // 2
    tiles = _own_tiles(core)
    own = np.concatenate([np.arange(t * 128, (t + 1) * 128) for t in tiles])
    xb = x[b]
    xfT = np.ascontiguousarray(xb.T)
    xoT = np.ascontiguousarray(xfT[:, own])

    inv_freq = 1.0 / 10000.0 ** (np.arange(0, HD, 2, dtype=np.float64) / HD)
    posk = np.arange(S, dtype=np.float64)
    pk = np.outer(inv_freq[np.arange(128) % 32], posk)
    ckK = np.cos(pk)
    skK = np.sin(pk) * np.where((np.arange(128) % 64) < 32, -1.0, 1.0)[:, None]
    qp = own.astype(np.float64)
    pq = np.outer(inv_freq[np.arange(128) % 64], qp)
    cqQ = np.cos(pq)
    sqQ = np.sin(pq) * np.where(np.arange(128) < 64, -1.0, 1.0)[:, None]

    d = dict(
        xfT=xfT, xoT=xoT,
        ckK=ckK.astype(BF), skK=skK.astype(BF),
        cqQ=cqQ.astype(BF), sqQ=sqQ.astype(BF),
        qposP=own.astype(np.float32)[None, :],
        kcolP=(np.arange(128)[:, None] + 128 * np.arange(NT)[None, :])
        .astype(np.float32),
    )
    d.update(w)
    return d, own


def _pm_tiles(WT, fm, mm):
    a = WT.reshape(fm, 128, mm, 128).transpose(1, 0, 2, 3)
    return np.ascontiguousarray(a.reshape(128, fm * mm * 128))


def _prep_weights(inp):
    Wq = np.asarray(inp["L0_attn_Wq"], np.float32)
    Wkr = np.asarray(inp["L0_attn_Wkr"], np.float32)
    Wdk = np.asarray(inp["L0_attn_Wdk"], np.float32)
    Wupk = np.asarray(inp["L0_attn_Wupk"], np.float32)
    Wupv = np.asarray(inp["L0_attn_Wupv"], np.float32)
    Wo = np.asarray(inp["L0_attn_Wo"], np.float32)

    wqP = np.ascontiguousarray(
        Wq.T.reshape(NT, 128, NT, 128).transpose(2, 1, 0, 3).reshape(4096, 4096))
    woP = np.ascontiguousarray(
        Wo.T.reshape(NT, 128, NT, 128).transpose(2, 1, 0, 3).reshape(4096, 4096))
    wdkP = _pm_tiles(np.ascontiguousarray(Wdk.T), NT, 4)
    wkrP = _pm_tiles(np.ascontiguousarray(Wkr.T), NT, 4)
    wupkP = _pm_tiles(np.ascontiguousarray(Wupk.T), 4, 4)
    wupvP = _pm_tiles(np.ascontiguousarray(Wupv.T), 4, 8)

    qg = np.asarray(inp["L0_ffn_qw_gate"]).astype(np.uint8).T
    qu = np.asarray(inp["L0_ffn_qw_up"]).astype(np.uint8).T
    qgP = qg.reshape(NT, 128, GUB, 256).transpose(2, 1, 0, 3)
    quP = qu.reshape(NT, 128, GUB, 256).transpose(2, 1, 0, 3)
    qguP = np.ascontiguousarray(
        np.stack([qgP, quP], axis=3).reshape(GUB * 128, 16384))

    def rows_T(a, blocks):
        return np.asarray(a, np.float32).T.reshape(NT, blocks, 256) \
            .transpose(1, 0, 2)

    zgT = rows_T(inp["L0_ffn_z_gate"], GUB)
    sgT = rows_T(inp["L0_ffn_s_gate"], GUB)
    zuT = rows_T(inp["L0_ffn_z_up"], GUB)
    suT = rows_T(inp["L0_ffn_s_up"], GUB)
    sclguP = np.ascontiguousarray(
        np.stack([zgT, sgT, zuT, suT], axis=2).reshape(GUB, 4 * NT * 256))

    qd = np.asarray(inp["L0_ffn_qw_down"]).astype(np.uint8).T
    qdP = np.ascontiguousarray(
        qd.reshape(IFT, 128, DB, 256).transpose(2, 1, 0, 3)
        .reshape(DB * 128, IFT * 256))
    zdT = np.asarray(inp["L0_ffn_z_down"], np.float32).T \
        .reshape(IFT, DB, 256).transpose(1, 0, 2)
    sdT = np.asarray(inp["L0_ffn_s_down"], np.float32).T \
        .reshape(IFT, DB, 256).transpose(1, 0, 2)
    scldP = np.ascontiguousarray(
        np.stack([zdT, sdT], axis=2).reshape(DB, 2 * IFT * 256))

    return dict(wqP=wqP, woP=woP, wdkP=wdkP, wkrP=wkrP, wupkP=wupkP,
                wupvP=wupvP, qguP=qguP, sclguP=sclguP, qdP=qdP, scldP=scldP)


def kernel(**inputs):
    x = np.asarray(inputs["L0_attn_hidden_states"], dtype=np.float32)
    w = _prep_weights(inputs)

    in_maps = []
    owns = []
    for core in range(8):
        d, own = _prep_core_inputs(core, x, w)
        in_maps.append(d)
        owns.append(own)

    nc = build_program()
    res = run_bass_kernel_spmd(nc, in_maps, core_ids=list(range(8)))

    out = np.empty((4, S, H), dtype=np.float32)
    for core in range(8):
        out[core // 2, owns[core], :] = res.results[core]["outT"].T
    return out


# version 05
# speedup vs baseline: 1.3322x; 1.1142x over previous
import math
import numpy as np
import ml_dtypes
import concourse.bass as bass
import concourse.mybir as mybir
from concourse.bass_utils import run_bass_kernel_spmd
from contextlib import ExitStack

F32 = mybir.dt.float32
BF16 = mybir.dt.bfloat16
U8 = mybir.dt.uint8
AF = mybir.ActivationFunctionType
ALU = mybir.AluOpType
AX = mybir.AxisListType
BF = ml_dtypes.bfloat16

H = 4096
S = 4096
R = 2048
NH, KVH, HD = 32, 8, 128
C = 512
RD = 512
I = 11008
NT = 32
IFT = 86
GUB = 43
DB = 16
STARTS_EVEN = [0, 12, 16, 28]
STARTS_ODD = [4, 8, 20, 24]
LIM = [8, 16, 24, 32]
M0 = [0, 8, 16, 24]
INVSQ = 1.0 / math.sqrt(HD)
LN_EPS = 1e-5
CW = 256

_DEBUG = []
_UNTIL = None


def _own_tiles(core):
    st = STARTS_EVEN if core % 2 == 0 else STARTS_ODD
    return [s + j for s in st for j in range(4)]


class Sem:
    def __init__(self, ctx, nc, name):
        self.h = ctx.enter_context(nc.semaphore(name))
        self.v = 0

    def inc(self, bi, n=1):
        bi.then_inc(self.h, n)
        self.v += n
        return bi


def build_program():
    nc = bass.Bass("TRN2", target_bir_lowering=False,
                   dynamic_dma_scratch_size=8192)

    def din(name, shape, dt):
        return nc.dram_tensor(name, list(shape), dt, kind="ExternalInput")

    xfT = din("xfT", (H, S), F32)
    xoT = din("xoT", (H, R), F32)
    wqP = din("wqP", (H, 4096), F32)
    woP = din("woP", (H, 4096), F32)
    wdkP = din("wdkP", (128, NT * C), F32)
    wkrP = din("wkrP", (128, NT * RD), F32)
    wupkP = din("wupkP", (128, 4 * C), F32)
    wupvP = din("wupvP", (128, 4 * 1024), F32)
    qguP = din("qguP", (GUB * 128, 16384), U8)
    sclguP = din("sclguP", (GUB, 4 * NT * 256), F32)
    qdP = din("qdP", (DB * 128, IFT * 256), U8)
    scldP = din("scldP", (DB, 2 * IFT * 256), F32)
    ckK = din("ckK", (128, S), BF16)
    skK = din("skK", (128, S), BF16)
    cqQ = din("cqQ", (128, R), BF16)
    sqQ = din("sqQ", (128, R), BF16)
    qposP = din("qposP", (1, R), F32)
    kcolP = din("kcolP", (128, NT), F32)

    outT = nc.dram_tensor("outT", [H, R], F32, kind="ExternalOutput")

    def scr(name, shape, dt):
        kind = "ExternalOutput" if name in _DEBUG else "Internal"
        return nc.dram_tensor(name, list(shape), dt, kind=kind)

    ckvS = scr("ckvS", (C, S), BF16)
    krS = scr("krS", (RD, S), BF16)
    ktS = scr("ktS", (KVH * 128, S), BF16)
    vS = scr("vS", (KVH * S, HD), BF16)
    qtS = scr("qtS", (4096, R), BF16)
    atS = scr("atS", (4096, R), BF16)
    stS = scr("stS", (H, R), BF16)
    hS = scr("hS", (I, R), BF16)
    wdS = scr("wdS", (NT * 128, I), BF16)
    sclgBF = scr("sclgBF", (GUB, NT * 1024), BF16)
    scldBF = scr("scldBF", (DB, 2 * IFT * 256), BF16)
    xnS = scr("xnS", (H, R), BF16)
    xnbS = scr("xnbS", (H, S), BF16)
    statS = scr("statS", (16, 1024), F32)
    wgS = scr("wgS", (H, I), BF16)
    wuS = scr("wuS", (H, I), BF16)
    biasS = scr("biasS", (128, 128), F32)

    ctx = ExitStack()
    ld = Sem(ctx, nc, "ld")
    stA = Sem(ctx, nc, "stA")
    stB = Sem(ctx, nc, "stB")
    stC = Sem(ctx, nc, "stC")
    stD = Sem(ctx, nc, "stD")
    pe = Sem(ctx, nc, "pe")
    act = Sem(ctx, nc, "act")
    dve = Sem(ctx, nc, "dve")
    lsc = [Sem(ctx, nc, f"lsc{i}") for i in range(2)]
    lqg = [Sem(ctx, nc, f"lqg{i}") for i in range(2)]
    lw = [Sem(ctx, nc, f"lw{i}") for i in range(2)]
    stq = [Sem(ctx, nc, f"stq{i}") for i in range(2)]
    stw = [Sem(ctx, nc, f"stw{i}") for i in range(2)]
    stv = [Sem(ctx, nc, f"stv{i}") for i in range(2)]

    def _finish():
        phase_end()
        with nc.Block("fin_dev") as blk:
            @blk.sync
            def _(sp):
                for s_ in [stA, stB, stC, stD, ld] + lsc + lqg + lw + stq + stw + stv:
                    sp.wait_ge(s_.h, s_.v)
        return nc

    def sb(name, shape, dt, stack):
        return stack.enter_context(nc.sbuf_tensor(name, list(shape), dt))

    def psf(name, shape, stack, dt=F32):
        return stack.enter_context(nc.psum_tensor(name, list(shape), dt))

    def load_bf16_via_f32(blk, dst, src_dram, stage, rows=slice(0, 128)):
        n = dst.shape[-1]
        half = stage.shape[-1]
        nchunks = (n + half - 1) // half
        for i in range(nchunks):
            c0, c1 = i * half, min((i + 1) * half, n)

            @blk.sync
            def _(sp, c0=c0, c1=c1):
                sp.wait_ge(dve.h, dve.v)
                ld.inc(sp.dma_start(stage[:, 0:c1 - c0],
                                    src_dram[rows, c0:c1]), 16)

            @blk.vector
            def _(v, c0=c0, c1=c1):
                v.wait_ge(ld.h, ld.v)
                dve.inc(v.tensor_copy(dst[:, c0:c1], stage[:, 0:c1 - c0]))

    g = ExitStack()
    ones_col = sb("ones_col", (128, 1), BF16, g)
    ones_row = sb("ones_row", (1, 128), BF16, g)
    ones_row_f = sb("ones_row_f", (1, 128), F32, g)
    ones128 = sb("ones128", (128, 128), BF16, g)
    norms = sb("norms", (1, 512), F32, g)
    bias_all = sb("bias_all", (128, 128), F32, g)
    cst = sb("cst", (1, 4), F32, g)
    cst128 = sb("cst128", (128, 1), F32, g)

    scope_state = [None, None]

    def phase(name):
        if scope_state[0] is not None:
            nc.leave_named_scope(scope_state[0], scope_state[1], False)
        sid, _ = nc.enter_named_scope(name, False)
        scope_state[0], scope_state[1] = name, sid

    def phase_end():
        if scope_state[0] is not None:
            nc.leave_named_scope(scope_state[0], scope_state[1], False)
            scope_state[0] = None

    with nc.Block("init") as blk:
        @blk.vector
        def _(v):
            v.memset(ones_col[:, :], 1.0)
            v.memset(ones_row[:, :], 1.0)
            v.memset(ones_row_f[:, :], 1.0)
            v.memset(ones128[:, :], 1.0)
            v.memset(norms[:, :], 0.0)
            v.memset(cst[0:1, 0:1], LN_EPS)
            v.memset(cst128[:, :], LN_EPS)
            v.memset(cst[0:1, 1:2], math.log(1.02 * INVSQ))

    phase("pscl")
    psc = ExitStack()
    scf32 = sb("scf32", (128, 11008), F32, psc)
    scb16 = sb("scb16", (128, 11008), BF16, psc)

    with nc.Block("pscl") as blk:
        @blk.sync
        def _(sp):
            ld.inc(sp.dma_start(
                scf32[:, :],
                bass.AP(sclguP, 0, [[11008, 128], [1, 11008]])), 16)

        @blk.vector
        def _(v):
            v.wait_ge(ld.h, ld.v)
            dve.inc(v.tensor_copy(scb16[:, :], scf32[:, :]))

        @blk.sync
        def _(sp):
            sp.wait_ge(dve.h, dve.v)
            stC.inc(sp.dma_start(
                bass.AP(sclgBF, 0, [[11008, 128], [1, 11008]]),
                scb16[:, :]), 16)

        @blk.sync
        def _(sp):
            sp.wait_ge(stC.h, stC.v)
            ld.inc(sp.dma_start(
                scf32[:, 0:5504],
                bass.AP(scldP, 0, [[5504, 128], [1, 5504]])), 16)

        @blk.vector
        def _(v):
            v.wait_ge(ld.h, ld.v)
            dve.inc(v.tensor_copy(scb16[:, 0:5504], scf32[:, 0:5504]))

        @blk.sync
        def _(sp):
            sp.wait_ge(dve.h, dve.v)
            stC.inc(sp.dma_start(
                bass.AP(scldBF, 0, [[5504, 128], [1, 5504]]),
                scb16[:, 0:5504]), 16)

        @blk.sync
        def _(sp):
            sp.wait_ge(stC.h, stC.v)

    psc.close()

    cvt_gate = [0]

    def emit_ln_chunk(blk, src_dram, c0, xf32, xch, sqt, m_b, i_b, ps_A, ps_B,
                      dst_fn):
        @blk.sync
        def _(sp):
            sp.wait_ge(dve.h, cvt_gate[0])
            for f in range(NT):
                ld.inc(sp.dma_start(xf32[:, f * CW:(f + 1) * CW],
                                    src_dram[f * 128:(f + 1) * 128, c0:c0 + CW]),
                       16)

        @blk.vector
        def _(v):
            v.wait_ge(ld.h, ld.v)
            v.wait_ge(dve.h, dve.v)
            for f in range(NT):
                bi = v.tensor_copy(xch[:, f * CW:(f + 1) * CW],
                                   xf32[:, f * CW:(f + 1) * CW])
            dve.inc(bi)

        cvt_dve = dve.v
        cvt_gate[0] = dve.v
        pe_b = pe.v
        act_b = act.v

        @blk.scalar
        def _(sc):
            sc.wait_ge(dve.h, cvt_dve)
            sc.wait_ge(pe.h, pe_b)
            for f in range(NT):
                act.inc(sc.activation(sqt[:, f * CW:(f + 1) * CW],
                                      xch[:, f * CW:(f + 1) * CW], AF.Square))

        @blk.tensor
        def _(t):
            t.wait_ge(dve.h, dve.v)
            for f in range(NT):
                t.matmul(ps_A[:, :], ones128[:, :],
                         xch[:, f * CW:(f + 1) * CW],
                         start=(f == 0), stop=(f == NT - 1))
            t.wait_ge(act.h, act_b + NT)
            for f in range(NT):
                pe.inc(t.matmul(ps_B[:, :], ones128[:, :],
                                sqt[:, f * CW:(f + 1) * CW],
                                start=(f == 0), stop=(f == NT - 1)))

        @blk.vector
        def _(v):
            v.wait_ge(pe.h, pe.v)
            dve.inc(v.tensor_scalar(m_b[:, :], ps_A[:, :], 1.0 / H, None,
                                    ALU.mult))
            dve.inc(v.tensor_scalar(i_b[:, :], ps_B[:, :], 1.0 / H, None,
                                    ALU.mult))
            v.wait_ge(dve.h, dve.v)
            dve.inc(v.scalar_tensor_tensor(sqt[:, 0:CW], m_b[:, :], -1.0,
                                           m_b[:, :], ALU.mult, ALU.mult))
            v.wait_ge(dve.h, dve.v)
            dve.inc(v.tensor_tensor(i_b[:, :], i_b[:, :], sqt[:, 0:CW],
                                    ALU.add))

        if "statS" in _DEBUG and src_dram.name == "xfT":
            @blk.sync
            def _(sp):
                sp.wait_ge(dve.h, dve.v)
                stA.inc(sp.dma_start(statS[c0 // CW:c0 // CW + 1, 0:256],
                                     m_b[0:1, :]), 16)
                stA.inc(sp.dma_start(statS[c0 // CW:c0 // CW + 1, 256:512],
                                     i_b[0:1, :]), 16)

        @blk.scalar
        def _(sc):
            sc.wait_ge(dve.h, dve.v)
            sc.wait_ge(stA.h, stA.v)
            act.inc(sc.activation(i_b[:, :], i_b[:, :], AF.Ln,
                                  bias=cst128[:, 0:1], scale=1.0))
            sc.wait_ge(act.h, act.v)
            act.inc(sc.activation(i_b[:, :], i_b[:, :], AF.Exp, scale=-0.5))

        if "statS" in _DEBUG and src_dram.name == "xfT":
            @blk.sync
            def _(sp):
                sp.wait_ge(act.h, act.v)
                stA.inc(sp.dma_start(statS[c0 // CW:c0 // CW + 1, 512:768],
                                     i_b[0:1, :]), 16)

            @blk.vector
            def _(v):
                v.wait_ge(stA.h, stA.v)
                dve.inc(v.tensor_copy(m_b[0:1, 0:1], m_b[0:1, 0:1]))

        @blk.vector
        def _(v):
            v.wait_ge(act.h, act.v)
            for f in range(NT):
                dst = dst_fn(f)
                dve.inc(v.tensor_tensor(dst, xch[:, f * CW:(f + 1) * CW],
                                        m_b[:, :], ALU.subtract))
                v.wait_ge(dve.h, dve.v)
                dve.inc(v.tensor_tensor(dst, dst, i_b[:, :], ALU.mult))

    p1 = ExitStack()
    wdk_sb = sb("wdk_sb", (128, NT * C), BF16, p1)
    wkr_sb = sb("wkr_sb", (128, NT * RD), BF16, p1)
    ck_sb = sb("ck_sb", (128, S), BF16, p1)
    sk_sb = sb("sk_sb", (128, S), BF16, p1)
    xf1 = sb("xf1", (128, NT * CW), F32, p1)
    xch1 = sb("xch1", (128, NT * CW), BF16, p1)
    xnch = sb("xnch", (128, NT * CW), BF16, p1)
    sqt1 = sb("sqt1", (128, NT * CW), BF16, p1)
    m_b1 = sb("m_b1", (128, CW), F32, p1)
    i_b1 = sb("i_b1", (128, CW), F32, p1)
    kvev = sb("kvev", (128, 4 * CW), BF16, p1)
    krev = sb("krev", (128, 4 * CW), BF16, p1)
    rp_t = sb("rp_t", (128, CW), BF16, p1)
    ps_A1 = psf("ps_A1", (128, CW), p1)
    ps_B1 = psf("ps_B1", (128, CW), p1)
    ps_kv = [psf(f"ps_kv{m}", (128, CW), p1) for m in range(4)]

    phase("p1")
    with nc.Block("p1_pre") as blk:
        @blk.sync
        def _(sp):
            ld.inc(sp.dma_start(ck_sb[:, :], ckK[:, :]), 16)
            ld.inc(sp.dma_start(sk_sb[:, :], skK[:, :]), 16)
        load_bf16_via_f32(blk, wdk_sb[:, :], wdkP, xf1)
        load_bf16_via_f32(blk, wkr_sb[:, :], wkrP, xf1)

    for ch in range(S // CW):
        c0 = ch * CW
        with nc.Block(f"p1_{ch}") as blk:
            emit_ln_chunk(blk, xfT, c0, xf1, xch1, sqt1, m_b1, i_b1,
                          ps_A1, ps_B1,
                          lambda f: xnch[:, f * CW:(f + 1) * CW])

        if "xnbS" in _DEBUG:
            with nc.Block(f"p1dbg_{ch}") as blk:
                @blk.sync
                def _(sp):
                    sp.wait_ge(dve.h, dve.v)
                    for f in range(NT):
                        stA.inc(sp.dma_start(
                            xnbS[f * 128:(f + 1) * 128, c0:c0 + CW],
                            xnch[:, f * CW:(f + 1) * CW]), 16)

        if _UNTIL == "p1ln":
            continue
        with nc.Block(f"p1kv_{ch}") as blk:
            @blk.tensor
            def _(t):
                t.wait_ge(dve.h, dve.v)
                for m in range(4):
                    for f in range(NT):
                        bi = t.matmul(
                            ps_kv[m][:, :],
                            wdk_sb[:, f * C + m * 128: f * C + (m + 1) * 128],
                            xnch[:, f * CW:(f + 1) * CW],
                            start=(f == 0), stop=(f == NT - 1))
                pe.inc(bi)

            @blk.vector
            def _(v):
                v.wait_ge(pe.h, pe.v)
                v.wait_ge(stA.h, stA.v)
                for m in range(4):
                    bi = v.tensor_copy(kvev[:, m * CW:(m + 1) * CW],
                                       ps_kv[m][:, :])
                dve.inc(bi)

            dve_kv = dve.v

            @blk.tensor
            def _(t):
                t.wait_ge(dve.h, dve_kv)
                for m in range(4):
                    for f in range(NT):
                        bi = t.matmul(
                            ps_kv[m][:, :],
                            wkr_sb[:, f * RD + m * 128: f * RD + (m + 1) * 128],
                            xnch[:, f * CW:(f + 1) * CW],
                            start=(f == 0), stop=(f == NT - 1))
                pe.inc(bi)

            @blk.sync
            def _(sp):
                sp.wait_ge(dve.h, dve_kv)
                for m in range(4):
                    stA.inc(sp.dma_start(
                        ckvS[m * 128:(m + 1) * 128, c0:c0 + CW],
                        kvev[:, m * CW:(m + 1) * CW]), 16)

            @blk.vector
            def _(v):
                v.wait_ge(pe.h, pe.v)
                v.wait_ge(stA.h, stA.v)
                for m in range(4):
                    for s32 in range(4):
                        q0 = s32 * 32
                        p0 = (s32 ^ 1) * 32
                        dst = krev[q0:q0 + 32, m * CW:(m + 1) * CW]
                        v.tensor_tensor(rp_t[q0:q0 + 32, :],
                                        ps_kv[m][p0:p0 + 32, :],
                                        sk_sb[q0:q0 + 32, c0:c0 + CW], ALU.mult)
                        dve.inc(v.tensor_tensor(dst, ps_kv[m][q0:q0 + 32, :],
                                                ck_sb[q0:q0 + 32, c0:c0 + CW],
                                                ALU.mult))
                        v.wait_ge(dve.h, dve.v)
                        dve.inc(v.tensor_tensor(dst, dst, rp_t[q0:q0 + 32, :],
                                                ALU.add))

            @blk.sync
            def _(sp):
                sp.wait_ge(dve.h, dve.v)
                for m in range(4):
                    stA.inc(sp.dma_start(
                        krS[m * 128:(m + 1) * 128, c0:c0 + CW],
                        krev[:, m * CW:(m + 1) * CW]), 16)

    p1.close()
    if _UNTIL in ("p1", "p1ln"):
        return _finish()

    p19 = ExitStack()
    ckv_sb = sb("ckv_sb", (128, 4 * S), BF16, p19)
    kr2_sb = sb("kr2_sb", (128, 4 * S), BF16, p19)
    wupk_sb = sb("wupk_sb", (128, 4 * C), BF16, p19)
    wupv_sb = sb("wupv_sb", (128, 4 * 1024), BF16, p19)
    kass = sb("kass", (128, S), BF16, p19)
    ksq = sb("ksq", (128, 2 * 512), BF16, p19)
    vev = sb("vev", (128, 2 * HD), BF16, p19)
    ps_kn = [psf(f"ps_kn{i}", (128, 512), p19) for i in range(2)]
    ps_v = [psf(f"ps_v{i}", (128, HD), p19) for i in range(2)]
    ps_nr = [psf(f"ps_nr{i}", (128, 512), p19) for i in range(2)]

    phase("p19")
    wst19 = sb("wst19", (128, 4096), F32, p19)
    with nc.Block("p19_pre") as blk:
        load_bf16_via_f32(blk, wupk_sb[:, :], wupkP, wst19)
        load_bf16_via_f32(blk, wupv_sb[:, :], wupvP, wst19)

        @blk.sync
        def _(sp):
            sp.wait_ge(stA.h, stA.v)
            for m in range(4):
                stA.inc(sp.dma_start(ckv_sb[:, m * S:(m + 1) * S],
                                     ckvS[m * 128:(m + 1) * 128, :]), 16)
                stA.inc(sp.dma_start(kr2_sb[:, m * S:(m + 1) * S],
                                     krS[m * 128:(m + 1) * 128, :]), 16)

    for h in range(KVH):
        with nc.Block(f"p19k_{h}") as blk:
            @blk.vector
            def _(v):
                v.wait_ge(stA.h, stA.v)
                v.wait_ge(ld.h, ld.v)
                b0 = 64 * (h % 2)
                hsl = slice((h // 2) * S, (h // 2 + 1) * S)
                v.tensor_copy(kass[0:32, :], kr2_sb[b0:b0 + 32, hsl])
                dve.inc(v.tensor_copy(kass[64:96, :],
                                      kr2_sb[b0 + 32:b0 + 64, hsl]))

            dve_cp = dve.v
            pe_b = pe.v

            @blk.tensor
            def _(t):
                t.wait_ge(stA.h, stA.v)
                t.wait_ge(ld.h, ld.v)
                mo = (h // 2) * 128 + (h % 2) * 64
                for nch in range(8):
                    if nch >= 2:
                        t.wait_ge(dve.h, dve_cp + (nch - 1))
                    for f in range(4):
                        bi = t.matmul(
                            ps_kn[nch % 2][0:64, :],
                            wupk_sb[:, f * C + mo: f * C + mo + 64],
                            ckv_sb[:, f * S + nch * 512: f * S + (nch + 1) * 512],
                            start=(f == 0), stop=(f == 3))
                        pe.inc(bi) if f == 3 else None

            @blk.vector
            def _(v):
                for nch in range(8):
                    v.wait_ge(pe.h, pe_b + nch + 1)
                    sl = slice(nch * 512, (nch + 1) * 512)
                    v.tensor_copy(kass[32:64, sl], ps_kn[nch % 2][0:32, :])
                    dve.inc(v.tensor_copy(kass[96:128, sl],
                                          ps_kn[nch % 2][32:64, :]))

            @blk.sync
            def _(sp):
                sp.wait_ge(dve.h, dve.v)
                stA.inc(sp.dma_start(ktS[h * 128:(h + 1) * 128, :], kass[:, :]),
                        16)

            act_b = act.v
            pe_b2 = pe.v
            dve_b2 = dve.v

            @blk.scalar
            def _(sc):
                sc.wait_ge(dve.h, dve_b2)
                for j in range(8):
                    if j >= 2:
                        sc.wait_ge(pe.h, pe_b2 + j - 1)
                    act.inc(sc.activation(ksq[:, (j % 2) * 512:(j % 2 + 1) * 512],
                                          kass[:, j * 512:(j + 1) * 512],
                                          AF.Square))

            @blk.tensor
            def _(t):
                for j in range(8):
                    t.wait_ge(act.h, act_b + j + 1)
                    if j >= 2:
                        t.wait_ge(dve.h, dve_b2 + j - 1)
                    pe.inc(t.matmul(ps_nr[j % 2][:, :], ones128[:, :],
                                    ksq[:, (j % 2) * 512:(j % 2 + 1) * 512],
                                    start=True, stop=True))

            @blk.vector
            def _(v):
                for j in range(8):
                    v.wait_ge(pe.h, pe_b2 + j + 1)
                    dve.inc(v.tensor_reduce(norms[0:1, 200 + j:201 + j],
                                            ps_nr[j % 2][0:1, :], AX.X, ALU.max))
                v.wait_ge(dve.h, dve.v)
                v.tensor_reduce(norms[0:1, h:h + 1],
                                norms[0:1, 200:208], AX.X, ALU.max)
                dve.inc(v.tensor_copy(norms[0:1, 220:221], norms[0:1, h:h + 1]))

        with nc.Block(f"p19v_{h}") as blk:
            stA_b = stA.v
            pe_b3 = pe.v
            dve_b3 = dve.v

            @blk.tensor
            def _(t):
                for kt in range(NT):
                    if kt >= 2:
                        t.wait_ge(dve.h, dve_b3 + (kt - 1))
                    for f in range(4):
                        bi = t.matmul(
                            ps_v[kt % 2][:, :],
                            ckv_sb[:, f * S + kt * 128: f * S + (kt + 1) * 128],
                            wupv_sb[:, f * 1024 + h * 128: f * 1024 + (h + 1) * 128],
                            start=(f == 0), stop=(f == 3))
                    pe.inc(bi)

            stv_b = [stv[0].v, stv[1].v]

            @blk.vector
            def _(v):
                for kt in range(NT):
                    v.wait_ge(pe.h, pe_b3 + kt + 1)
                    if kt >= 2:
                        v.wait_ge(stv[kt % 2].h, stv_b[kt % 2] + 16 * (kt // 2))
                    dve.inc(v.tensor_copy(
                        vev[:, (kt % 2) * HD:(kt % 2 + 1) * HD],
                        ps_v[kt % 2][:, :]))

            @blk.sync
            def _(sp):
                dve_b4 = dve.v - NT
                for kt in range(NT):
                    sp.wait_ge(dve.h, dve_b4 + kt + 1)
                    stv[kt % 2].inc(sp.dma_start(
                        vS[h * S + kt * 128: h * S + (kt + 1) * 128, :],
                        vev[:, (kt % 2) * HD:(kt % 2 + 1) * HD]), 16)

    p19.close()
    if _UNTIL == "p19":
        return _finish()

    gx = ExitStack()
    xn_own = sb("xn_own", (128, NT * R), BF16, gx)
    p15 = ExitStack()
    p15a = ExitStack()
    xf1b = sb("xf1b", (128, NT * CW), F32, p15a)
    xch2 = sb("xch2", (128, NT * CW), BF16, p15a)
    sqt2 = sb("sqt2", (128, NT * CW), BF16, p15a)
    m_b2 = sb("m_b2", (128, CW), F32, p15a)
    i_b2 = sb("i_b2", (128, CW), F32, p15a)
    ps_A2 = psf("ps_A2", (128, CW), p15a)
    ps_B2 = psf("ps_B2", (128, CW), p15a)

    phase("p15ln")
    for ch in range(R // CW):
        c0 = ch * CW
        with nc.Block(f"p15_{ch}") as blk:
            emit_ln_chunk(blk, xoT, c0, xf1b, xch2, sqt2, m_b2, i_b2,
                          ps_A2, ps_B2,
                          lambda f: xn_own[:, f * R + c0: f * R + c0 + CW])

    if "xnS" in _DEBUG:
        with nc.Block("p15dbg") as blk:
            @blk.sync
            def _(sp):
                sp.wait_ge(dve.h, dve.v)
                for f in range(NT):
                    stA.inc(sp.dma_start(xnS[f * 128:(f + 1) * 128, :],
                                         xn_own[:, f * R:(f + 1) * R]), 16)

    p15a.close()
    wq_sb = [sb(f"wq_sb{i}", (128, H), BF16, p15) for i in range(2)]
    q_ev = [sb(f"q_ev{i}", (128, 512), BF16, p15) for i in range(2)]
    q_t1 = sb("q_t1", (128, 512), BF16, p15)
    qsq2 = sb("qsq2", (128, 512), BF16, p15)
    cq_sb = sb("cq_sb", (128, R), BF16, p15)
    sq_sb = sb("sq_sb", (128, R), BF16, p15)
    ps_q = [psf(f"ps_q{i}", (128, 512), p15) for i in range(2)]
    ps_qn = [psf(f"ps_qn{i}", (128, 512), p15) for i in range(2)]

    phase("p15q")
    wstq = [sb(f"wstq{i}", (128, H), F32, p15) for i in range(2)]
    with nc.Block("p15q_pre") as blk:
        @blk.sync
        def _(sp):
            ld.inc(sp.dma_start(cq_sb[:, :], cqQ[:, :]), 16)
            ld.inc(sp.dma_start(sq_sb[:, :], sqQ[:, :]), 16)
            ld.inc(sp.dma_start(wstq[0][:, :], wqP[0:128, :]), 16)

        @blk.vector
        def _(v):
            v.wait_ge(ld.h, ld.v)
            dve.inc(v.tensor_copy(wq_sb[0][:, :], wstq[0][:, :]))

    wq_cvt = {0: dve.v, 1: 0}

    for m in range(NT):
        with nc.Block(f"p15q_{m}") as blk:
            @blk.sync
            def _(sp):
                if m + 1 < NT:
                    sp.wait_ge(dve.h, dve.v)
                    ld.inc(sp.dma_start(wstq[(m + 1) % 2][:, :],
                                        wqP[(m + 1) * 128:(m + 2) * 128, :]), 16)

            if m + 1 < NT:
                @blk.vector
                def _(v):
                    v.wait_ge(ld.h, ld.v)
                    v.wait_ge(pe.h, pe.v)
                    dve.inc(v.tensor_copy(wq_sb[(m + 1) % 2][:, :],
                                          wstq[(m + 1) % 2][:, :]))
                wq_cvt[(m + 1) % 2] = dve.v
            pe_b = pe.v
            dve_b = dve.v
            act_b = act.v
            stq_b = [stq[0].v, stq[1].v]

            @blk.tensor
            def _(t):
                t.wait_ge(dve.h, wq_cvt[m % 2])
                for cch in range(4):
                    if cch >= 2:
                        t.wait_ge(dve.h, dve_b + 3 * (cch - 1))
                    for f in range(NT):
                        bi = t.matmul(
                            ps_q[cch % 2][:, :],
                            wq_sb[m % 2][:, f * 128:(f + 1) * 128],
                            xn_own[:, f * R + cch * 512: f * R + cch * 512 + 512],
                            start=(f == 0), stop=(f == NT - 1))
                    pe.inc(bi)

            @blk.vector
            def _(v):
                for cch in range(4):
                    v.wait_ge(pe.h, pe_b + cch + 1)
                    if cch >= 2:
                        v.wait_ge(stq[cch % 2].h, stq_b[cch % 2] + 16)
                    pq = ps_q[cch % 2]
                    qe = q_ev[cch % 2]
                    rsl = slice(cch * 512, cch * 512 + 512)
                    for s64 in range(2):
                        a0, b0 = s64 * 64, (1 - s64) * 64
                        v.tensor_tensor(q_t1[a0:a0 + 64, :], pq[b0:b0 + 64, :],
                                        sq_sb[a0:a0 + 64, rsl], ALU.mult)
                        dve.inc(v.tensor_tensor(qe[a0:a0 + 64, :],
                                                pq[a0:a0 + 64, :],
                                                cq_sb[a0:a0 + 64, rsl],
                                                ALU.mult))
                        v.wait_ge(dve.h, dve.v)
                        if s64 == 0:
                            v.tensor_tensor(qe[a0:a0 + 64, :], qe[a0:a0 + 64, :],
                                            q_t1[a0:a0 + 64, :], ALU.add)
                        else:
                            dve.inc(v.tensor_tensor(qe[a0:a0 + 64, :],
                                                    qe[a0:a0 + 64, :],
                                                    q_t1[a0:a0 + 64, :],
                                                    ALU.add))

            @blk.sync
            def _(sp):
                for cch in range(4):
                    sp.wait_ge(dve.h, dve_b + 3 * (cch + 1))
                    stq[cch % 2].inc(sp.dma_start(
                        qtS[m * 128:(m + 1) * 128, cch * 512:cch * 512 + 512],
                        q_ev[cch % 2][:, :]), 16)

            @blk.scalar
            def _(sc):
                for cch in range(4):
                    sc.wait_ge(dve.h, dve_b + 3 * (cch + 1))
                    if cch >= 1:
                        sc.wait_ge(pe.h, pe_b + 4 + cch)
                    act.inc(sc.activation(qsq2[:, :], q_ev[cch % 2][:, :],
                                          AF.Square))

            @blk.tensor
            def _(t):
                qn_dve_b = dve_b + 12
                for cch in range(4):
                    t.wait_ge(act.h, act_b + cch + 1)
                    if cch >= 2:
                        t.wait_ge(dve.h, qn_dve_b + cch - 1)
                    pe.inc(t.matmul(ps_qn[cch % 2][:, :], ones128[:, :],
                                    qsq2[:, :], start=True, stop=True))

            @blk.vector
            def _(v):
                pe_c = pe_b + 4
                for cch in range(4):
                    v.wait_ge(pe.h, pe_c + cch + 1)
                    dve.inc(v.tensor_reduce(
                        norms[0:1, 8 + m * 4 + cch: 9 + m * 4 + cch],
                        ps_qn[cch % 2][0:1, :], AX.X, ALU.max))


    phase("pbias")
    ps_bb = psf("ps_bb", (128, 128), p15)
    with nc.Block("pbias") as blk:
        @blk.scalar
        def _(sc):
            sc.wait_ge(dve.h, dve.v)
            sc.activation(norms[0:1, 136:264], norms[0:1, 8:136], AF.Ln)
            act.inc(sc.activation(norms[0:1, 300:308], norms[0:1, 0:8], AF.Ln))

        @blk.vector
        def _(v):
            v.wait_ge(act.h, act.v)
            for h in range(8):
                bi = v.tensor_scalar(norms[0:1, 136 + 16 * h:136 + 16 * (h + 1)],
                                     norms[0:1, 136 + 16 * h:136 + 16 * (h + 1)],
                                     norms[0:1, 300 + h:301 + h], None, ALU.add)
            dve.inc(bi)

        @blk.scalar
        def _(sc):
            sc.wait_ge(dve.h, dve.v)
            act.inc(sc.activation(norms[0:1, 136:264], norms[0:1, 136:264],
                                  AF.Exp, scale=0.5, bias=cst[0:1, 1:2]))

        @blk.vector
        def _(v):
            v.wait_ge(act.h, act.v)
            dve.inc(v.tensor_scalar(norms[0:1, 136:264], norms[0:1, 136:264],
                                    -1.0, None, ALU.mult))

        @blk.tensor
        def _(t):
            t.wait_ge(dve.h, dve.v)
            pe.inc(t.matmul(ps_bb[:, :], ones_row_f[:, :], norms[0:1, 136:264],
                            start=True, stop=True))

        @blk.vector
        def _(v):
            v.wait_ge(pe.h, pe.v)
            dve.inc(v.tensor_copy(bias_all[:, :], ps_bb[:, :]))

        if "biasS" in _DEBUG:
            @blk.sync
            def _(sp):
                sp.wait_ge(dve.h, dve.v)
                stA.inc(sp.dma_start(biasS[:, :], bias_all[:, :]), 16)

    p15.close()
    if _UNTIL == "p15":
        return _finish()

    p3a = ExitStack()
    qgu_sb = [sb(f"qgu_sb{i}", (128, 16384), U8, p3a) for i in range(2)]
    scl_bf = [sb(f"scl_bf{i}", (128, 2048), BF16, p3a) for i in range(2)]
    wg_sb = sb("wg_sb", (128, NT * 256), BF16, p3a)
    wu_sb = sb("wu_sb", (128, NT * 256), BF16, p3a)
    dq_t = sb("dq_t", (128, 256), BF16, p3a)
    sg_tmp = sb("sg_tmp", (128, R), BF16, p3a)
    h_ev = sb("h_ev", (128, R), BF16, p3a)
    ps_gu = psf("ps_gu", (128, R), p3a)

    dq_done = [0, 0]

    phase("p3a")
    with nc.Block("p3a_pre") as blk:
        @blk.sync
        def _(sp):
            lqg[0].inc(sp.dma_start(qgu_sb[0][:, :], qguP[0:128, :]), 16)

    for b in range(GUB):
        bi_ = b % 2
        with nc.Block(f"p3a_{b}") as blk:
            dve_b = dve.v
            scl_at = []

            scl_at = []

            @blk.scalar
            def _(sc):
                for g2 in range(NT // 2):
                    if g2 >= 2:
                        sc.wait_ge(dve.h, dve_b + 4 * 2 * (g2 - 1))
                    lsc[g2 % 2].inc(sc.dma_start(
                        scl_bf[g2 % 2][:, :],
                        bass.AP(sclgBF, b * (NT * 1024) + g2 * 2048,
                                [[0, 128], [1, 2048]])), 16)
                    scl_at.append(lsc[g2 % 2].v)

            @blk.sync
            def _(sp):
                if b + 1 < GUB:
                    sp.wait_ge(dve.h, dq_done[(b + 1) % 2])
                    lqg[(b + 1) % 2].inc(sp.dma_start(
                        qgu_sb[(b + 1) % 2][:, :],
                        qguP[(b + 1) * 128:(b + 2) * 128, :]), 16)

            pe_prev = pe.v

            @blk.vector
            def _(v):
                v.wait_ge(pe.h, pe_prev)
                v.wait_ge(lqg[bi_].h, lqg[bi_].v)
                for f in range(NT):
                    v.wait_ge(lsc[(f // 2) % 2].h, scl_at[f // 2])
                    sc_ = scl_bf[(f // 2) % 2]
                    o2 = (f % 2) * 1024
                    qg_ = qgu_sb[bi_][:, f * 512:f * 512 + 256]
                    dve.inc(v.tensor_tensor(dq_t[:, :], qg_,
                                            sc_[:, o2:o2 + 256], ALU.subtract))
                    v.wait_ge(dve.h, dve.v)
                    dve.inc(v.tensor_tensor(wg_sb[:, f * 256:(f + 1) * 256],
                                            dq_t[:, :], sc_[:, o2 + 256:o2 + 512],
                                            ALU.mult))
                    qu_ = qgu_sb[bi_][:, f * 512 + 256:f * 512 + 512]
                    dve.inc(v.tensor_tensor(dq_t[:, :], qu_,
                                            sc_[:, o2 + 512:o2 + 768],
                                            ALU.subtract))
                    v.wait_ge(dve.h, dve.v)
                    dve.inc(v.tensor_tensor(wu_sb[:, f * 256:(f + 1) * 256],
                                            dq_t[:, :], sc_[:, o2 + 768:o2 + 1024],
                                            ALU.mult))

            dq_done[bi_] = dve.v
            dq_v = dve.v

            for gidx, (mat, wsrc, msub) in enumerate(
                    ((0, wg_sb, 0), (1, wu_sb, 0), (0, wg_sb, 1), (1, wu_sb, 1))):

                @blk.tensor
                def _(t, gidx=gidx, mat=mat, wsrc=wsrc, msub=msub):
                    if mat == 0:
                        t.wait_ge(dve.h, dve.v)
                    else:
                        t.wait_ge(act.h, act.v)
                    for f in range(NT):
                        if gidx == 0:
                            t.wait_ge(dve.h, dve_b + 4 * f + 2)
                        elif gidx == 1:
                            t.wait_ge(dve.h, dve_b + 4 * f + 4)
                        for nch in range(4):
                            bi2 = t.matmul(
                                ps_gu[:, nch * 512:(nch + 1) * 512],
                                wsrc[:, f * 256 + msub * 128:
                                     f * 256 + msub * 128 + 128],
                                xn_own[:, f * R + nch * 512:
                                       f * R + (nch + 1) * 512],
                                start=(f == 0), stop=(f == NT - 1))
                    pe.inc(bi2)

                if mat == 0:
                    @blk.scalar
                    def _(sc):
                        sc.wait_ge(pe.h, pe.v)
                        act.inc(sc.activation(sg_tmp[:, :], ps_gu[:, :],
                                              AF.Silu))
                else:
                    @blk.vector
                    def _(v):
                        v.wait_ge(pe.h, pe.v)
                        v.wait_ge(stB.h, stB.v)
                        dve.inc(v.tensor_tensor(h_ev[:, :], sg_tmp[:, :],
                                                ps_gu[:, :], ALU.mult))

                    @blk.sync
                    def _(sp, msub=msub):
                        sp.wait_ge(dve.h, dve.v)
                        irow = b * 256 + msub * 128
                        stB.inc(sp.dma_start(hS[irow:irow + 128, :],
                                             h_ev[:, :]), 16)

    p3a.close()
    gx.close()
    if _UNTIL == "p3a":
        return _finish()

    p2 = ExitStack()
    kt_sb = [sb(f"kt_sb{i}", (128, S), BF16, p2) for i in range(2)]
    vh_sb = [sb(f"vh_sb{i}", (128, S), BF16, p2) for i in range(2)]
    qt_sb = [sb(f"qt_sb{i}", (128, 512), BF16, p2) for i in range(2)]
    pt_sb = [sb(f"pt_sb{i}", (128, 512), BF16, p2) for i in range(2)]
    mask_sb = sb("mask_sb", (128, 8 * 512), BF16, p2)
    at_sb = sb("at_sb", (128, NH * 512), BF16, p2)
    il_sb = sb("il_sb", (128, 512), F32, p2)
    qpos_sb = sb("qpos_sb", (1, R), F32, p2)
    kcol_sb = sb("kcol_sb", (128, NT), F32, p2)
    ps_st = [psf(f"ps_st{i}", (128, 512), p2) for i in range(3)]
    ps_o = psf("ps_o", (128, 512), p2)
    ps_l = psf("ps_l", (128, 512), p2)
    ps_x = psf("ps_x", (128, 512), p2)

    phase("p2")
    with nc.Block("p2_pre") as blk:
        @blk.sync
        def _(sp):
            ld.inc(sp.dma_start(qpos_sb[:, :], qposP[:, :]), 16)
            ld.inc(sp.dma_start(kcol_sb[:, :], kcolP[:, :]), 16)

    for cch in range(4):
        L = LIM[cch]
        m0 = M0[cch]
        with nc.Block(f"p2m_{cch}") as blk:
            @blk.tensor
            def _(t):
                t.wait_ge(ld.h, ld.v)
                t.wait_ge(dve.h, dve.v)
                pe.inc(t.matmul(ps_x[:, :], ones_row_f[:, :],
                                qpos_sb[0:1, cch * 512:(cch + 1) * 512],
                                start=True, stop=True))

            @blk.vector
            def _(v):
                v.wait_ge(pe.h, pe.v)
                v.wait_ge(stB.h, stB.v)
                for kt in range(m0, L):
                    bi = v.tensor_scalar(
                        mask_sb[:, (kt - m0) * 512:(kt - m0 + 1) * 512],
                        ps_x[:, :], kcol_sb[:, kt:kt + 1], None, ALU.is_ge)
                dve.inc(bi)

        for kvh in range(KVH):
            kbi = kvh % 2
            with nc.Block(f"p2kv_{cch}_{kvh}") as blk:
                @blk.sync
                def _(sp):
                    sp.wait_ge(pe.h, pe.v)
                    stB.inc(sp.dma_start(
                        kt_sb[kbi][:, 0:L * 128],
                        ktS[kvh * 128:(kvh + 1) * 128, 0:L * 128]), 16)
                    for kt in range(L):
                        stB.inc(sp.dma_start(
                            vh_sb[kbi][:, kt * 128:(kt + 1) * 128],
                            vS[kvh * S + kt * 128: kvh * S + (kt + 1) * 128, :]),
                            16)

            for qh in range(4):
                head = kvh * 4 + qh
                qbi = head % 2
                slot = head * 4 + cch
                with nc.Block(f"p2h_{cch}_{head}") as blk:
                    @blk.sync
                    def _(sp):
                        sp.wait_ge(pe.h, pe.v)
                        stB.inc(sp.dma_start(
                            qt_sb[qbi][:, :],
                            qtS[head * 128:(head + 1) * 128,
                                cch * 512:(cch + 1) * 512]), 16)

                    pe_b = pe.v
                    act_b = act.v
                    dve_b = dve.v

                    def st_done(kt):
                        return pe_b + (1 if kt == 0 else 3 * kt - 1)

                    def pv_done(kt):
                        return pe_b + 3 * (kt + 1)

                    @blk.tensor
                    def _(t):
                        t.wait_ge(stB.h, stB.v)
                        t.wait_ge(dve.h, dve_b)
                        for it in range(L + 1):
                            if it < L:
                                kt = it
                                if kt >= 3:
                                    t.wait_ge(act.h, act_b + kt - 2)
                                pe.inc(t.matmul(
                                    ps_st[kt % 3][:, :],
                                    kt_sb[kbi][:, kt * 128:(kt + 1) * 128],
                                    qt_sb[qbi][:, :], start=True, stop=True))
                            if it >= 1:
                                kt = it - 1
                                if kt < m0:
                                    t.wait_ge(act.h, act_b + kt + 1)
                                else:
                                    t.wait_ge(dve.h, dve_b + (kt - m0) + 1)
                                pe.inc(t.matmul(
                                    ps_o[:, :],
                                    vh_sb[kbi][:, kt * 128:(kt + 1) * 128],
                                    pt_sb[kt % 2][:, :],
                                    start=(kt == 0), stop=(kt == L - 1)))
                                pe.inc(t.matmul(
                                    ps_l[:, :], ones128[:, :],
                                    pt_sb[kt % 2][:, :],
                                    start=(kt == 0), stop=(kt == L - 1)))

                    @blk.scalar
                    def _(sc):
                        for kt in range(L):
                            sc.wait_ge(pe.h, st_done(kt))
                            if kt >= 2:
                                sc.wait_ge(pe.h, pv_done(kt - 2))
                            act.inc(sc.activation(
                                pt_sb[kt % 2][:, :], ps_st[kt % 3][:, :],
                                AF.Exp, bias=bias_all[:, slot:slot + 1],
                                scale=INVSQ))

                    @blk.vector
                    def _(v):
                        for kt in range(m0, L):
                            v.wait_ge(act.h, act_b + kt + 1)
                            dve.inc(v.tensor_tensor(
                                pt_sb[kt % 2][:, :], pt_sb[kt % 2][:, :],
                                mask_sb[:, (kt - m0) * 512:(kt - m0 + 1) * 512],
                                ALU.mult))

                    pe_tail = pe_b + 3 * L

                    @blk.vector
                    def _(v):
                        v.wait_ge(pe.h, pe_tail)
                        dve.inc(v.reciprocal(il_sb[:, :], ps_l[:, :]))
                        v.wait_ge(dve.h, dve.v)
                        dve.inc(v.tensor_tensor(
                            at_sb[:, head * 512:(head + 1) * 512], ps_o[:, :],
                            il_sb[:, :], ALU.mult))

        with nc.Block(f"p2st_{cch}") as blk:
            @blk.sync
            def _(sp):
                sp.wait_ge(dve.h, dve.v)
                for hh in range(NH):
                    stB.inc(sp.dma_start(
                        atS[hh * 128:(hh + 1) * 128, cch * 512:(cch + 1) * 512],
                        at_sb[:, hh * 512:(hh + 1) * 512]), 16)

    p2.close()
    if _UNTIL == "p2":
        return _finish()

    p25 = ExitStack()
    at2_sb = sb("at2_sb", (128, NH * R), BF16, p25)
    wo_sb = [sb(f"wo_sb{i}", (128, H), BF16, p25) for i in range(2)]
    wsto = [sb(f"wsto{i}", (128, H), F32, p25) for i in range(2)]
    st_ev = [sb(f"st_ev{i}", (128, 512), BF16, p25) for i in range(2)]
    ps_w = [psf(f"ps_w{i}", (128, 512), p25) for i in range(2)]

    phase("p25")
    with nc.Block("p25_pre") as blk:
        @blk.sync
        def _(sp):
            sp.wait_ge(stB.h, stB.v)
            for hh in range(NH):
                stC.inc(sp.dma_start(at2_sb[:, hh * R:(hh + 1) * R],
                                     atS[hh * 128:(hh + 1) * 128, :]), 16)

        @blk.scalar
        def _(sc):
            ld.inc(sc.dma_start(wsto[0][:, :], woP[0:128, :]), 16)

        @blk.vector
        def _(v):
            v.wait_ge(ld.h, ld.v)
            dve.inc(v.tensor_copy(wo_sb[0][:, :], wsto[0][:, :]))

    wo_cvt = {0: dve.v, 1: 0}

    for m in range(NT):
        with nc.Block(f"p25_{m}") as blk:
            @blk.scalar
            def _(sc):
                if m + 1 < NT:
                    sc.wait_ge(dve.h, dve.v)
                    ld.inc(sc.dma_start(wsto[(m + 1) % 2][:, :],
                                        woP[(m + 1) * 128:(m + 2) * 128, :]), 16)

            if m + 1 < NT:
                @blk.vector
                def _(v):
                    v.wait_ge(ld.h, ld.v)
                    v.wait_ge(pe.h, pe.v)
                    dve.inc(v.tensor_copy(wo_sb[(m + 1) % 2][:, :],
                                          wsto[(m + 1) % 2][:, :]))
                wo_cvt[(m + 1) % 2] = dve.v

            pe_b = pe.v
            dve_b = dve.v
            stC_b = stC.v

            @blk.tensor
            def _(t):
                t.wait_ge(dve.h, wo_cvt[m % 2])
                t.wait_ge(stC.h, stC_b)
                for cch in range(4):
                    if cch >= 2:
                        t.wait_ge(dve.h, dve_b + cch - 1)
                    for f in range(NH):
                        bi = t.matmul(
                            ps_w[cch % 2][:, :],
                            wo_sb[m % 2][:, f * 128:(f + 1) * 128],
                            at2_sb[:, f * R + cch * 512: f * R + cch * 512 + 512],
                            start=(f == 0), stop=(f == NH - 1))
                    pe.inc(bi)

            stw_b = [stw[0].v, stw[1].v]

            @blk.vector
            def _(v):
                for cch in range(4):
                    v.wait_ge(pe.h, pe_b + cch + 1)
                    if cch >= 2:
                        v.wait_ge(stw[cch % 2].h, stw_b[cch % 2] + 16)
                    dve.inc(v.tensor_copy(st_ev[cch % 2][:, :],
                                          ps_w[cch % 2][:, :]))

            @blk.sync
            def _(sp):
                for cch in range(4):
                    sp.wait_ge(dve.h, dve_b + cch + 1)
                    stw[cch % 2].inc(sp.dma_start(
                        stS[m * 128:(m + 1) * 128, cch * 512:(cch + 1) * 512],
                        st_ev[cch % 2][:, :]), 16)

    p25.close()
    if _UNTIL == "p25":
        return _finish()

    p3d = ExitStack()
    GD = 4
    NGD = (IFT + GD - 1) // GD
    qd_sb = [sb(f"qd_sb{i}", (128, IFT * 256), U8, p3d) for i in range(2)]
    scd_bf = [sb(f"scd_bf{i}", (128, 2048), BF16, p3d) for i in range(2)]
    wd_ev = sb("wd_ev", (128, IFT * 256), BF16, p3d)
    dq_t2 = sb("dq_t2", (128, 256), BF16, p3d)

    dqd_done = [0, 0]
    phase("p3d")
    with nc.Block("p3d_pre") as blk:
        @blk.sync
        def _(sp):
            lqg[0].inc(sp.dma_start(qd_sb[0][:, :], qdP[0:128, :]), 16)

    for b in range(DB):
        bi_ = b % 2
        with nc.Block(f"p3d_{b}") as blk:
            dve_b = dve.v
            scl_at = []

            scl_at = []

            @blk.scalar
            def _(sc):
                for g4 in range((IFT + 3) // 4):
                    nf = min(4, IFT - g4 * 4)
                    if g4 >= 2:
                        sc.wait_ge(dve.h, dve_b + 2 * 4 * (g4 - 1))
                    lsc[g4 % 2].inc(sc.dma_start(
                        scd_bf[g4 % 2][:, 0:nf * 512],
                        bass.AP(scldBF, b * (2 * IFT * 256) + g4 * 2048,
                                [[0, 128], [1, nf * 512]])), 16)
                    scl_at.append(lsc[g4 % 2].v)

            @blk.sync
            def _(sp):
                if b + 1 < DB:
                    sp.wait_ge(dve.h, dqd_done[(b + 1) % 2])
                    lqg[(b + 1) % 2].inc(sp.dma_start(
                        qd_sb[(b + 1) % 2][:, :],
                        qdP[(b + 1) * 128:(b + 2) * 128, :]), 16)

            stC_b = stC.v

            @blk.vector
            def _(v):
                v.wait_ge(stC.h, stC_b)
                v.wait_ge(lqg[bi_].h, lqg[bi_].v)
                for f in range(IFT):
                    v.wait_ge(lsc[(f // 4) % 2].h, scl_at[f // 4])
                    sc_ = scd_bf[(f // 4) % 2]
                    o4 = (f % 4) * 512
                    qsl = qd_sb[bi_][:, f * 256:(f + 1) * 256]
                    dve.inc(v.tensor_tensor(dq_t2[:, :], qsl,
                                            sc_[:, o4:o4 + 256], ALU.subtract))
                    v.wait_ge(dve.h, dve.v)
                    dve.inc(v.tensor_tensor(wd_ev[:, f * 256:(f + 1) * 256],
                                            dq_t2[:, :], sc_[:, o4 + 256:o4 + 512],
                                            ALU.mult))

            dqd_done[bi_] = dve.v
            dve_b2 = dve.v

            @blk.sync
            def _(sp):
                sp.wait_ge(dve.h, dve_b2)
                for msub in range(2):
                    m = b * 2 + msub
                    for f0 in range(0, IFT, 11):
                        f1 = min(f0 + 11, IFT)
                        src = bass.AP(wd_ev, msub * 128 + f0 * 256,
                                      [[IFT * 256, 128], [256, f1 - f0], [1, 128]])
                        stC.inc(sp.dma_start(
                            wdS[m * 128:(m + 1) * 128, f0 * 128:f1 * 128], src),
                            16)

    p3d.close()
    if _UNTIL == "p3d":
        return _finish()

    p3b = ExitStack()
    h_sb = sb("h_sb", (128, IFT * 512), BF16, p3b)
    wd_sb = [sb(f"wd_sb{i}", (128, I), BF16, p3b) for i in range(2)]
    st_in = [sb(f"st_in{i}", (128, 512), BF16, p3b) for i in range(2)]
    x_in = [sb(f"x_in{i}", (128, 512), F32, p3b) for i in range(2)]
    gf_t = sb("gf_t", (128, 512), BF16, p3b)
    o_ev = [sb(f"o_ev{i}", (128, 512), F32, p3b) for i in range(2)]
    ps_d = [psf(f"ps_d{i}", (128, 512), p3b) for i in range(2)]

    phase("p3b")
    for cch in range(4):
        with nc.Block(f"p3b_{cch}") as blk:
            @blk.sync
            def _(sp):
                sp.wait_ge(pe.h, pe.v)
                for f in range(IFT):
                    stD.inc(sp.dma_start(
                        h_sb[:, f * 512:(f + 1) * 512],
                        hS[f * 128:(f + 1) * 128, cch * 512:(cch + 1) * 512]),
                        16)

            @blk.scalar
            def _(sc):
                sc.wait_ge(pe.h, pe.v)
                lw[0].inc(sc.dma_start(wd_sb[0][:, :], wdS[0:128, :]), 16)
                lw[0].inc(sc.dma_start(st_in[0][:, :],
                                       stS[0:128, cch * 512:(cch + 1) * 512]), 16)
                lw[0].inc(sc.dma_start(x_in[0][:, :],
                                       xoT[0:128, cch * 512:(cch + 1) * 512]), 16)

        act_b3 = act.v
        dve_b3 = dve.v
        stD_h_loads = stD.v
        for m in range(NT):
            with nc.Block(f"p3b_{cch}_{m}") as blk:
                @blk.scalar
                def _(sc):
                    if m + 1 < NT:
                        sc.wait_ge(pe.h, pe.v)
                        par = (m + 1) % 2
                        lw[par].inc(sc.dma_start(
                            wd_sb[par][:, :],
                            wdS[(m + 1) * 128:(m + 2) * 128, :]), 16)
                        lw[par].inc(sc.dma_start(
                            st_in[par][:, :],
                            stS[(m + 1) * 128:(m + 2) * 128,
                                cch * 512:(cch + 1) * 512]), 16)
                        lw[par].inc(sc.dma_start(
                            x_in[par][:, :],
                            xoT[(m + 1) * 128:(m + 2) * 128,
                                cch * 512:(cch + 1) * 512]), 16)

                lw_need = lw[m % 2].v

                @blk.tensor
                def _(t):
                    t.wait_ge(lw[m % 2].h, lw_need)
                    t.wait_ge(stD.h, stD_h_loads)
                    if m >= 2:
                        t.wait_ge(act.h, act_b3 + m - 1)
                    for f in range(IFT):
                        bi = t.matmul(ps_d[m % 2][:, :],
                                      wd_sb[m % 2][:, f * 128:(f + 1) * 128],
                                      h_sb[:, f * 512:(f + 1) * 512],
                                      start=(f == 0), stop=(f == IFT - 1))
                    pe.inc(bi)

                @blk.scalar
                def _(sc):
                    sc.wait_ge(pe.h, pe.v)
                    if m >= 1:
                        sc.wait_ge(dve.h, dve_b3 + 2 * m)
                    act.inc(sc.activation(gf_t[:, :], ps_d[m % 2][:, :],
                                          AF.Silu))

                @blk.vector
                def _(v):
                    v.wait_ge(act.h, act_b3 + m + 1)
                    if m >= 2:
                        v.wait_ge(stD.h, stD_h_loads + 16 * m)
                    dve.inc(v.tensor_tensor(gf_t[:, :], st_in[m % 2][:, :],
                                            gf_t[:, :], ALU.mult))
                    v.wait_ge(dve.h, dve.v)
                    dve.inc(v.tensor_tensor(o_ev[m % 2][:, :], x_in[m % 2][:, :],
                                            gf_t[:, :], ALU.add))

                @blk.sync
                def _(sp):
                    sp.wait_ge(dve.h, dve.v)
                    stD.inc(sp.dma_start(
                        outT[m * 128:(m + 1) * 128, cch * 512:(cch + 1) * 512],
                        o_ev[m % 2][:, :]), 16)

    phase_end()
    with nc.Block("fin") as blk:
        @blk.sync
        def _(sp):
            sp.wait_ge(stD.h, stD.v)
            sp.wait_ge(ld.h, ld.v)
            for s_ in lsc + lqg + lw + stq + stw + stv:
                sp.wait_ge(s_.h, s_.v)

    p3b.close()
    g.close()
    ctx.close()
    return nc


def _prep_core_inputs(core, x, w):
    b = core // 2
    tiles = _own_tiles(core)
    own = np.concatenate([np.arange(t * 128, (t + 1) * 128) for t in tiles])
    xb = x[b]
    xfT = np.ascontiguousarray(xb.T)
    xoT = np.ascontiguousarray(xfT[:, own])

    inv_freq = 1.0 / 10000.0 ** (np.arange(0, HD, 2, dtype=np.float64) / HD)
    posk = np.arange(S, dtype=np.float64)
    pk = np.outer(inv_freq[np.arange(128) % 32], posk)
    ckK = np.cos(pk)
    skK = np.sin(pk) * np.where((np.arange(128) % 64) < 32, -1.0, 1.0)[:, None]
    qp = own.astype(np.float64)
    pq = np.outer(inv_freq[np.arange(128) % 64], qp)
    cqQ = np.cos(pq)
    sqQ = np.sin(pq) * np.where(np.arange(128) < 64, -1.0, 1.0)[:, None]

    d = dict(
        xfT=xfT, xoT=xoT,
        ckK=ckK.astype(BF), skK=skK.astype(BF),
        cqQ=cqQ.astype(BF), sqQ=sqQ.astype(BF),
        qposP=own.astype(np.float32)[None, :],
        kcolP=(np.arange(128)[:, None] + 128 * np.arange(NT)[None, :])
        .astype(np.float32),
    )
    d.update(w)
    return d, own


def _pm_tiles(WT, fm, mm):
    a = WT.reshape(fm, 128, mm, 128).transpose(1, 0, 2, 3)
    return np.ascontiguousarray(a.reshape(128, fm * mm * 128))


def _prep_weights(inp):
    Wq = np.asarray(inp["L0_attn_Wq"], np.float32)
    Wkr = np.asarray(inp["L0_attn_Wkr"], np.float32)
    Wdk = np.asarray(inp["L0_attn_Wdk"], np.float32)
    Wupk = np.asarray(inp["L0_attn_Wupk"], np.float32)
    Wupv = np.asarray(inp["L0_attn_Wupv"], np.float32)
    Wo = np.asarray(inp["L0_attn_Wo"], np.float32)

    wqP = np.ascontiguousarray(
        Wq.T.reshape(NT, 128, NT, 128).transpose(2, 1, 0, 3).reshape(4096, 4096))
    woP = np.ascontiguousarray(
        Wo.T.reshape(NT, 128, NT, 128).transpose(2, 1, 0, 3).reshape(4096, 4096))
    wdkP = _pm_tiles(np.ascontiguousarray(Wdk.T), NT, 4)
    wkrP = _pm_tiles(np.ascontiguousarray(Wkr.T), NT, 4)
    wupkP = _pm_tiles(np.ascontiguousarray(Wupk.T), 4, 4)
    wupvP = _pm_tiles(np.ascontiguousarray(Wupv.T), 4, 8)

    qg = np.asarray(inp["L0_ffn_qw_gate"]).astype(np.uint8).T
    qu = np.asarray(inp["L0_ffn_qw_up"]).astype(np.uint8).T
    qgP = qg.reshape(NT, 128, GUB, 256).transpose(2, 1, 0, 3)
    quP = qu.reshape(NT, 128, GUB, 256).transpose(2, 1, 0, 3)
    qguP = np.ascontiguousarray(
        np.stack([qgP, quP], axis=3).reshape(GUB * 128, 16384))

    def rows_T(a, blocks):
        return np.asarray(a, np.float32).T.reshape(NT, blocks, 256) \
            .transpose(1, 0, 2)

    zgT = rows_T(inp["L0_ffn_z_gate"], GUB)
    sgT = rows_T(inp["L0_ffn_s_gate"], GUB)
    zuT = rows_T(inp["L0_ffn_z_up"], GUB)
    suT = rows_T(inp["L0_ffn_s_up"], GUB)
    sclguP = np.ascontiguousarray(
        np.stack([zgT, sgT, zuT, suT], axis=2).reshape(GUB, 4 * NT * 256))

    qd = np.asarray(inp["L0_ffn_qw_down"]).astype(np.uint8).T
    qdP = np.ascontiguousarray(
        qd.reshape(IFT, 128, DB, 256).transpose(2, 1, 0, 3)
        .reshape(DB * 128, IFT * 256))
    zdT = np.asarray(inp["L0_ffn_z_down"], np.float32).T \
        .reshape(IFT, DB, 256).transpose(1, 0, 2)
    sdT = np.asarray(inp["L0_ffn_s_down"], np.float32).T \
        .reshape(IFT, DB, 256).transpose(1, 0, 2)
    scldP = np.ascontiguousarray(
        np.stack([zdT, sdT], axis=2).reshape(DB, 2 * IFT * 256))

    return dict(wqP=wqP, woP=woP, wdkP=wdkP, wkrP=wkrP, wupkP=wupkP,
                wupvP=wupvP, qguP=qguP, sclguP=sclguP, qdP=qdP, scldP=scldP)


def kernel(**inputs):
    x = np.asarray(inputs["L0_attn_hidden_states"], dtype=np.float32)
    w = _prep_weights(inputs)

    in_maps = []
    owns = []
    for core in range(8):
        d, own = _prep_core_inputs(core, x, w)
        in_maps.append(d)
        owns.append(own)

    nc = build_program()
    res = run_bass_kernel_spmd(nc, in_maps, core_ids=list(range(8)))

    out = np.empty((4, S, H), dtype=np.float32)
    for core in range(8):
        out[core // 2, owns[core], :] = res.results[core]["outT"].T
    return out
